# version 18
# baseline (speedup 1.0000x reference)
import numpy as np
import ml_dtypes
import concourse.bass as bass
import concourse.mybir as mybir
from concourse.bass_utils import run_bass_kernel_spmd
from contextlib import ExitStack

F32 = mybir.dt.float32
BF16 = mybir.dt.bfloat16
AX = mybir.AxisListType.X
ALU = mybir.AluOpType
ACT = mybir.ActivationFunctionType

T = 2048
C = 2048
NT = 16
NCC = 16
NTC = 4
H = 16
KV = 4
D = 128
NE = 8
NHC = 8
WIN = 4
EPS = 1e-5

ENGS = ("sync", "gpsimd", "tensor", "vector", "scalar")


class MK:
    def __init__(self):
        self.nc = bass.Bass("TRN2", target_bir_lowering=False)
        self.ops = {k: [] for k in ENGS}
        self.cnt = {s: 0 for s in ("ld", "wt", "st", "te", "ve", "se")}
        self.semh = {}
        self.lastwait = {}
        self.pguard = {}
        self.pprod = {}

    def w(self, eng, *tokens):
        flat = []
        for t in tokens:
            if t is None:
                continue
            if isinstance(t, list):
                flat += [u for u in t if u is not None]
            else:
                flat.append(t)
        for s, v in flat:
            if v <= 0:
                continue
            assert v <= self.cnt[s], f"future token {s}={v} > {self.cnt[s]}"
            key = (eng, s)
            if self.lastwait.get(key, -1) >= v:
                continue
            self.lastwait[key] = v
            self.ops[eng].append(lambda E, s=s, v=v: E.wait_ge(self.semh[s], v))

    def do(self, eng, make, sem=None, k=1):
        if sem is None:
            self.ops[eng].append(lambda E, m=make: m(E))
            return None
        self.cnt[sem] += k
        v = self.cnt[sem]

        def f(E, m=make, s=sem, kk=k):
            m(E).then_inc(self.semh[s], kk)
        self.ops[eng].append(f)
        return (sem, v)

    def tok(self, sem):
        return (sem, self.cnt[sem])

    def mm_into(self, regions):
        self.w("tensor", *[self.pguard.get(r) for r in regions])

    def mm_done(self, regions, tok):
        for r in regions:
            self.pprod[r] = tok

    def evac_wait(self, eng, regions):
        self.w(eng, *[self.pprod.get(r) for r in regions])

    def evac_done(self, regions, tok):
        for r in regions:
            self.pguard[r] = tok


def build_program():
    mk = MK()
    nc = mk.nc

    d_x = nc.dram_tensor("x", [T, C], F32, kind="ExternalInput").ap()
    d_wq = nc.dram_tensor("wqT", [C, H * D], F32, kind="ExternalInput").ap()
    d_wk = nc.dram_tensor("wkT", [C, KV * D], F32, kind="ExternalInput").ap()
    d_wv = nc.dram_tensor("wvT", [C, KV * D], F32, kind="ExternalInput").ap()
    d_wo = nc.dram_tensor("woT", [H * D, C], F32, kind="ExternalInput").ap()
    d_gw = nc.dram_tensor("gw9", [C, 9], F32, kind="ExternalInput").ap()
    d_wg = nc.dram_tensor("wgT", [NE, C, 1024], F32, kind="ExternalInput").ap()
    d_wu = nc.dram_tensor("wuT", [NE, C, 1024], F32, kind="ExternalInput").ap()
    d_wd = nc.dram_tensor("wdT", [NE, 1024, C], F32, kind="ExternalInput").ap()
    d_shg = nc.dram_tensor("shgT", [C, 2048], F32, kind="ExternalInput").ap()
    d_shu = nc.dram_tensor("shuT", [C, 2048], F32, kind="ExternalInput").ap()
    d_shd = nc.dram_tensor("shdT", [2048, C], F32, kind="ExternalInput").ap()
    d_ident = nc.dram_tensor("ident", [128, 128], BF16, kind="ExternalInput").ap()
    d_ctab = nc.dram_tensor("ctab", [128, T], BF16, kind="ExternalInput").ap()
    d_stab = nc.dram_tensor("stab", [128, T], BF16, kind="ExternalInput").ap()
    d_masks = nc.dram_tensor("masks", [128, 256], BF16, kind="ExternalInput").ap()
    d_out = nc.dram_tensor("out", [T, C], F32, kind="ExternalOutput").ap()
    d_acc = nc.dram_tensor("acc_spill", [T, C], BF16, kind="Internal").ap()
    d_q = nc.dram_tensor("q_scr", [H, 128, T], BF16, kind="Internal").ap()

    with ExitStack() as ctx:
        e = ctx.enter_context
        A_xn = e(nc.sbuf_tensor("A_xn", [128, NCC * T], BF16))
        A_big = e(nc.sbuf_tensor("A_big", [128, NCC * T], BF16))
        A_s1 = e(nc.sbuf_tensor("A_s1", [128, 16384], BF16))
        A_s2 = e(nc.sbuf_tensor("A_s2", [128, 16384], BF16))
        s_ident = e(nc.sbuf_tensor("s_ident", [128, 128], BF16))
        s_ctab = e(nc.sbuf_tensor("s_ctab", [128, T], BF16))
        s_stab = e(nc.sbuf_tensor("s_stab", [128, T], BF16))
        s_masks = e(nc.sbuf_tensor("s_masks", [128, 256], BF16))
        s_gw = e(nc.sbuf_tensor("s_gw", [128, NCC * 9], BF16))
        s_stats = e(nc.sbuf_tensor("s_stats", [128, 8], F32))
        s_sc9 = e(nc.sbuf_tensor("s_sc9", [128, 16], F32))
        s_top8 = e(nc.sbuf_tensor("s_top8", [128, 8], F32))
        s_dd = e(nc.sbuf_tensor("s_dd", [128, 2], F32))
        s_ee = e(nc.sbuf_tensor("s_ee", [128, 2], F32))
        s_w12 = e(nc.sbuf_tensor("s_w12", [128, 2], F32))
        s_eq = e(nc.sbuf_tensor("s_eq", [128, 16], F32))
        s_comb = e(nc.sbuf_tensor("s_comb", [128, NT * NE], F32))
        s_sgate = e(nc.sbuf_tensor("s_sgate", [128, NT], F32))
        s_nmax = e(nc.sbuf_tensor("s_nmax", [128, 4], F32))
        s_sum = e(nc.sbuf_tensor("s_sum", [128, 4], F32))
        s_inv = e(nc.sbuf_tensor("s_inv", [128, 4], F32))
        s_tmp = e(nc.sbuf_tensor("s_tmp", [128, 1024], BF16))
        s_dg = e(nc.sbuf_tensor("s_dg", [128, 384], BF16))

        PS1 = e(nc.psum_tensor("PS1", [128, 1024], F32))
        PS2 = e(nc.psum_tensor("PS2", [128, 1024], F32))
        PT = e(nc.psum_tensor("PT", [128, 512], F32))
        PTbf = None
        PAV = e(nc.psum_tensor("PAV", [128, 512], F32))
        PO = e(nc.psum_tensor("PO", [128, 1024], F32))

        for s in ("ld", "wt", "st", "te", "ve", "se"):
            mk.semh[s] = e(nc.semaphore(s))

        xn = A_xn[:].rearrange("p (c t) -> p c t", c=NCC)
        acc = A_big[:].rearrange("p (t o) -> p t o", t=NT)
        gwv = s_gw[:].rearrange("p (c n) -> p c n", c=NCC)
        s2f = A_s2[:].bitcast(F32)
        PS1bf = PS1[:].bitcast(BF16)
        PTbf = PT[:].bitcast(BF16)

        D_ = mk.do
        W_ = mk.w

        t_ident = D_("sync", lambda E: E.dma_start(out=s_ident[:], in_=d_ident), "ld", 16)
        t_ctab = D_("sync", lambda E: E.dma_start(out=s_ctab[:], in_=d_ctab), "ld", 16)
        t_stab = D_("sync", lambda E: E.dma_start(out=s_stab[:], in_=d_stab), "ld", 16)
        t_masks = D_("sync", lambda E: E.dma_start(out=s_masks[:], in_=d_masks), "ld", 16)
        t_gw = None
        for cc in range(NCC):
            t_gw = D_("gpsimd", lambda E, cc=cc: E.dma_start(
                out=gwv[:, cc, :], in_=d_gw[128 * cc:128 * (cc + 1), :]), "wt", 16)

        xbuf = [s2f[:, 0:2048], s2f[:, 2048:4096]]
        sqd = s2f[:, 4096:6144]
        xnt = [A_s2[:, 12288:14336], A_s2[:, 14336:16384]]
        xbuf_free = [None, None]
        xnt_free = [None, None]
        tr_dst = [PTbf[:, 0:512], PS1bf[:, 0:512]]
        tr_reg = [["pt"], ["ps1a"]]
        acc_init = [None] * NT
        gate_tok = [None] * NT
        sc9_free = None

        for tt in range(NT):
            b = tt % 2
            W_("sync", xbuf_free[b])
            t_x = D_("sync", lambda E, tt=tt, b=b: E.dma_start(
                out=xbuf[b], in_=d_x[128 * tt:128 * (tt + 1), :]), "ld", 16)
            acc_init[tt] = D_("gpsimd", lambda E, tt=tt: E.dma_start(
                out=acc[:, tt, :], in_=d_x[128 * tt:128 * (tt + 1), :]), "wt", 16)
            W_("scalar", t_x)
            t_sq = D_("scalar", lambda E, b=b: E.activation(
                sqd, xbuf[b], ACT.Square, accum_out=s_stats[:, 1:2]), "se")
            t_sm = D_("scalar", lambda E, b=b: E.activation(
                sqd, xbuf[b], ACT.Identity, accum_out=s_stats[:, 0:1]), "se")
            W_("vector", t_sq, t_sm)
            t_v = D_("vector", lambda E: E.tensor_scalar(
                s_stats[:, 2:3], s_stats[:, 0:1], -1.0 / C, None, ALU.mult), "ve")
            W_("vector", t_v)
            t_v = D_("vector", lambda E: E.tensor_tensor(
                s_stats[:, 3:4], s_stats[:, 2:3], s_stats[:, 2:3], ALU.mult), "ve")
            t_v2 = D_("vector", lambda E: E.tensor_scalar(
                s_stats[:, 4:5], s_stats[:, 1:2], 1.0 / C, None, ALU.mult), "ve")
            W_("vector", t_v, t_v2)
            t_v = D_("vector", lambda E: E.tensor_tensor(
                s_stats[:, 4:5], s_stats[:, 4:5], s_stats[:, 3:4], ALU.subtract), "ve")
            W_("vector", t_v)
            t_v = D_("vector", lambda E: E.tensor_scalar_add(
                s_stats[:, 4:5], s_stats[:, 4:5], EPS), "ve")
            W_("scalar", t_v)
            t_s = D_("scalar", lambda E: E.activation(
                s_stats[:, 5:6], s_stats[:, 4:5], ACT.Ln), "se")
            W_("scalar", t_s)
            t_s = D_("scalar", lambda E: E.activation(
                s_stats[:, 6:7], s_stats[:, 5:6], ACT.Exp, scale=-0.5), "se")
            W_("vector", t_s)
            t_v = D_("vector", lambda E: E.tensor_tensor(
                s_stats[:, 7:8], s_stats[:, 2:3], s_stats[:, 6:7], ALU.mult), "ve")
            W_("scalar", t_v, xnt_free[b])
            t_xn = D_("scalar", lambda E, b=b: E.activation(
                xnt[b], xbuf[b], ACT.Identity, bias=s_stats[:, 7:8],
                scale=s_stats[:, 6:7]), "se")
            xbuf_free[b] = t_xn
            last_evac = None
            for g in range(4):
                rg = tr_reg[g % 2]
                mk.mm_into(rg)
                W_("tensor", t_xn, t_ident)
                t_t = None
                for j in range(4):
                    cc = 4 * g + j
                    t_t = D_("tensor", lambda E, b=b, cc=cc, g=g, j=j: E.transpose(
                        tr_dst[g % 2][:, 128 * j:128 * (j + 1)],
                        xnt[b][:, 128 * cc:128 * (cc + 1)], s_ident[:]),
                        "te" if j == 3 else None)
                mk.mm_done(rg, t_t)
                mk.evac_wait("vector", rg)
                t_ev = D_("vector", lambda E, g=g, tt=tt: E.tensor_copy(
                    xn[:, 4 * g:4 * g + 4, 128 * tt:128 * (tt + 1)],
                    tr_dst[g % 2][:].rearrange("p (c t) -> p c t", c=4)), "ve")
                mk.evac_done(rg, t_ev)
                last_evac = t_ev
            xnt_free[b] = mk.tok("te")
            mk.mm_into(["pav"])
            W_("tensor", last_evac, t_gw)
            t_mm = None
            for cc in range(NCC):
                t_mm = D_("tensor", lambda E, cc=cc, tt=tt: E.matmul(
                    PAV[:, 0:9], xn[:, cc, 128 * tt:128 * (tt + 1)], gwv[:, cc, :],
                    start=(cc == 0), stop=(cc == NCC - 1)),
                    "te" if cc == NCC - 1 else None)
            mk.mm_done(["pav"], t_mm)
            mk.evac_wait("vector", ["pav"])
            W_("vector", sc9_free)
            t_cp = D_("vector", lambda E: E.tensor_copy(s_sc9[:, 0:9], PAV[:, 0:9]), "ve")
            mk.evac_done(["pav"], t_cp)
            W_("vector", t_cp)
            t_m8 = D_("vector", lambda E: E.max(out=s_top8[:], in_=s_sc9[:, 0:8]), "ve")
            W_("vector", t_m8)
            t_d = D_("vector", lambda E: E.tensor_tensor(
                s_dd[:, 0:1], s_top8[:, 0:1], s_top8[:, 1:2], ALU.subtract), "ve")
            W_("vector", t_d)
            t_d2 = D_("vector", lambda E: E.tensor_scalar(
                s_dd[:, 1:2], s_dd[:, 0:1], -1.0, None, ALU.mult), "ve")
            W_("scalar", t_m8, t_d2)
            t_se1 = D_("scalar", lambda E: E.activation(
                s_ee[:, 0:2], s_dd[:, 0:2], ACT.Exp, scale=-1.0), "se")
            t_se2 = D_("scalar", lambda E: E.activation(
                s_sc9[:, 9:10], s_sc9[:, 8:9], ACT.Exp, scale=-1.0), "se")
            sc9_free = t_se2
            W_("vector", t_se1, t_se2)
            t_v = D_("vector", lambda E: E.tensor_scalar_add(
                s_ee[:, 0:2], s_ee[:, 0:2], 1.0), "ve")
            t_v2 = D_("vector", lambda E: E.tensor_scalar_add(
                s_sc9[:, 9:10], s_sc9[:, 9:10], 1.0), "ve")
            W_("vector", t_v, t_v2)
            t_w = D_("vector", lambda E: E.reciprocal(s_w12[:, 0:2], s_ee[:, 0:2]), "ve")
            t_sg = D_("vector", lambda E, tt=tt: E.reciprocal(
                s_sgate[:, tt:tt + 1], s_sc9[:, 9:10]), "ve")
            W_("vector", t_w, t_sg)
            t_e1 = D_("vector", lambda E: E.tensor_single_scalar(
                s_eq[:, 0:8], s_sc9[:, 0:8], s_top8[:, 0:1], ALU.is_equal), "ve")
            t_e2 = D_("vector", lambda E: E.tensor_single_scalar(
                s_eq[:, 8:16], s_sc9[:, 0:8], s_top8[:, 1:2], ALU.is_equal), "ve")
            W_("vector", t_e1, t_e2)
            t_v = D_("vector", lambda E, tt=tt: E.tensor_scalar(
                s_comb[:, 8 * tt:8 * tt + 8], s_eq[:, 0:8], s_w12[:, 0:1], None,
                ALU.mult), "ve")
            W_("vector", t_v)
            gate_tok[tt] = D_("vector", lambda E, tt=tt: E.scalar_tensor_tensor(
                s_comb[:, 8 * tt:8 * tt + 8], s_eq[:, 8:16], s_w12[:, 1:2],
                s_comb[:, 8 * tt:8 * tt + 8], ALU.mult, ALU.add), "ve")

        xn_done = mk.tok("ve")
        p1_done_te = mk.tok("te")
        p1_done_se = mk.tok("se")
        acc_ready = mk.tok("wt")

        shg_b = [A_s2[:, 0:2048], A_s2[:, 2048:4096]]
        shu_b = [A_s2[:, 4096:6144], A_s2[:, 6144:8192]]
        shd_b = [A_s2[:, 8192:12288], A_s2[:, 12288:16384]]
        hid = A_s1[:].rearrange("p (h t) -> p h t", h=8)
        gu_reg = [[["ps1a"], ["ps1b"]], [["ps2a"], ["ps2b"]]]
        dn_reg = [["po0"], ["po1"]]
        dn_dst = [PO[:, 0:512], PO[:, 512:1024]]
        shg_free = [[xn_done, p1_done_te, p1_done_se], [xn_done, p1_done_te, p1_done_se]]
        shu_free = [list(shg_free[0]), list(shg_free[1])]
        shd_free = [list(shg_free[0]), list(shg_free[1])]
        stp_free = [None, None]
        hid_wtok = None

        srcg_sh = d_shg.rearrange("(cc p) n -> p cc n", p=128)
        srcu_sh = d_shu.rearrange("(cc p) n -> p cc n", p=128)
        srcd_sh = d_shd.rearrange("(hh p) n -> p hh n", p=128)

        for half in range(2):
            for hc in range(8):
                hsc = 8 * half + hc
                b = hc % 2
                W_("gpsimd", shg_free[b], shu_free[b])
                t_g = t_u = None
                for q4 in range(4):
                    t_g = D_("gpsimd", lambda E, hsc=hsc, q4=q4, b=b: E.dma_start(
                        out=shg_b[b].rearrange("p (cc n) -> p cc n", cc=NCC)[:, 4 * q4:4 * q4 + 4, :],
                        in_=srcg_sh[:, 4 * q4:4 * q4 + 4, 128 * hsc:128 * (hsc + 1)]), "wt", 16)
                    t_u = D_("gpsimd", lambda E, hsc=hsc, q4=q4, b=b: E.dma_start(
                        out=shu_b[b].rearrange("p (cc n) -> p cc n", cc=NCC)[:, 4 * q4:4 * q4 + 4, :],
                        in_=srcu_sh[:, 4 * q4:4 * q4 + 4, 128 * hsc:128 * (hsc + 1)]), "wt", 16)
                for tc in range(NTC):
                    par = (hc * NTC + tc) % 2
                    rgu = gu_reg[0][par] + gu_reg[1][par]
                    pg = PS1[:, 512 * par:512 * (par + 1)]
                    pu = PS2[:, 512 * par:512 * (par + 1)]
                    mk.mm_into(rgu)
                    W_("tensor", t_g, t_u, xn_done)
                    t_mm = None
                    for cc in range(NCC):
                        D_("tensor", lambda E, cc=cc, tc=tc, b=b, pg=pg: E.matmul(
                            pg, shg_b[b][:, 128 * cc:128 * (cc + 1)],
                            xn[:, cc, 512 * tc:512 * (tc + 1)],
                            start=(cc == 0), stop=(cc == NCC - 1)))
                    for cc in range(NCC):
                        t_mm = D_("tensor", lambda E, cc=cc, tc=tc, b=b, pu=pu: E.matmul(
                            pu, shu_b[b][:, 128 * cc:128 * (cc + 1)],
                            xn[:, cc, 512 * tc:512 * (tc + 1)],
                            start=(cc == 0), stop=(cc == NCC - 1)),
                            "te" if cc == NCC - 1 else None)
                    mk.mm_done(rgu, t_mm)
                    if tc == NTC - 1:
                        shg_free[b] = [t_mm]
                        shu_free[b] = [t_mm]
                    stp = s_tmp[:, 512 * par:512 * (par + 1)]
                    mk.evac_wait("scalar", gu_reg[0][par])
                    W_("scalar", stp_free[par])
                    t_s = D_("scalar", lambda E, stp=stp, pg=pg: E.activation(
                        stp, pg, ACT.Silu), "se")
                    mk.evac_done(gu_reg[0][par], t_s)
                    mk.evac_wait("vector", gu_reg[1][par])
                    W_("vector", t_s)
                    hid_wtok = D_("vector", lambda E, hc=hc, tc=tc, stp=stp, pu=pu: E.tensor_tensor(
                        hid[:, hc, 512 * tc:512 * (tc + 1)], stp, pu, ALU.mult), "ve")
                    mk.evac_done(gu_reg[1][par], hid_wtok)
                    stp_free[par] = hid_wtok
            hid_done = hid_wtok
            for oc in range(NTC):
                b = oc % 2
                W_("gpsimd", shd_free[b])
                t_d = None
                for q2 in range(2):
                    t_d = D_("gpsimd", lambda E, half=half, oc=oc, b=b, q2=q2: E.dma_start(
                        out=shd_b[b].rearrange("p (hh n) -> p hh n", hh=8)[:, 4 * q2:4 * q2 + 4, :],
                        in_=srcd_sh[:, 8 * half + 4 * q2:8 * half + 4 * q2 + 4,
                                    512 * oc:512 * (oc + 1)]), "wt", 16)
                for tt in range(NT):
                    par = (oc * NT + tt) % 2
                    rr = dn_reg[par]
                    pd = dn_dst[par]
                    mk.mm_into(rr)
                    W_("tensor", t_d, hid_done)
                    t_mm = None
                    for hc in range(8):
                        t_mm = D_("tensor", lambda E, hc=hc, tt=tt, b=b, pd=pd: E.matmul(
                            pd[:, 0:512], hid[:, hc, 128 * tt:128 * (tt + 1)],
                            shd_b[b][:, 512 * hc:512 * (hc + 1)],
                            start=(hc == 0), stop=(hc == 7)),
                            "te" if hc == 7 else None)
                    mk.mm_done(rr, t_mm)
                    if tt == NT - 1:
                        shd_free[b] = [t_mm]
                    mk.evac_wait("vector", rr)
                    W_("vector", gate_tok[tt], acc_ready)
                    t_v = D_("vector", lambda E, tt=tt, oc=oc, pd=pd: E.scalar_tensor_tensor(
                        acc[:, tt, 512 * oc:512 * (oc + 1)], pd[:, 0:512],
                        s_sgate[:, tt:tt + 1], acc[:, tt, 512 * oc:512 * (oc + 1)],
                        ALU.mult, ALU.add), "ve")
                    mk.evac_done(rr, t_v)
        shared_done_te = mk.tok("te")

        wg_b = shg_b
        wu_b = shu_b
        wd_b = shd_b
        hh = hid
        wg_free = [[shared_done_te], [shared_done_te]]
        wu_free = [[shared_done_te], [shared_done_te]]
        wd_free = [[shared_done_te], [shared_done_te]]
        h_free = [shared_done_te]
        h_done = None
        srcg_e = d_wg.rearrange("e (cc p) n -> e p cc n", p=128)
        srcu_e = d_wu.rearrange("e (cc p) n -> e p cc n", p=128)
        srcd_e = d_wd.rearrange("e (hh p) n -> e p hh n", p=128)

        for ex in range(NE):
            for hc in range(NHC):
                b = hc % 2
                W_("gpsimd", wg_free[b], wu_free[b])
                t_g = t_u = None
                for q4 in range(4):
                    t_g = D_("gpsimd", lambda E, ex=ex, hc=hc, q4=q4, b=b: E.dma_start(
                        out=wg_b[b].rearrange("p (cc n) -> p cc n", cc=NCC)[:, 4 * q4:4 * q4 + 4, :],
                        in_=srcg_e[ex, :, 4 * q4:4 * q4 + 4, 128 * hc:128 * (hc + 1)]), "wt", 16)
                    t_u = D_("gpsimd", lambda E, ex=ex, hc=hc, q4=q4, b=b: E.dma_start(
                        out=wu_b[b].rearrange("p (cc n) -> p cc n", cc=NCC)[:, 4 * q4:4 * q4 + 4, :],
                        in_=srcu_e[ex, :, 4 * q4:4 * q4 + 4, 128 * hc:128 * (hc + 1)]), "wt", 16)
                for tc in range(NTC):
                    par = (hc * NTC + tc) % 2
                    rgu = gu_reg[0][par] + gu_reg[1][par]
                    pg = PS1[:, 512 * par:512 * (par + 1)]
                    pu = PS2[:, 512 * par:512 * (par + 1)]
                    mk.mm_into(rgu)
                    W_("tensor", t_g, t_u, h_free)
                    t_mm = None
                    for cc in range(NCC):
                        D_("tensor", lambda E, cc=cc, tc=tc, b=b, pg=pg: E.matmul(
                            pg, wg_b[b][:, 128 * cc:128 * (cc + 1)],
                            xn[:, cc, 512 * tc:512 * (tc + 1)],
                            start=(cc == 0), stop=(cc == NCC - 1)))
                    for cc in range(NCC):
                        t_mm = D_("tensor", lambda E, cc=cc, tc=tc, b=b, pu=pu: E.matmul(
                            pu, wu_b[b][:, 128 * cc:128 * (cc + 1)],
                            xn[:, cc, 512 * tc:512 * (tc + 1)],
                            start=(cc == 0), stop=(cc == NCC - 1)),
                            "te" if cc == NCC - 1 else None)
                    mk.mm_done(rgu, t_mm)
                    if tc == NTC - 1:
                        wg_free[b] = [t_mm]
                        wu_free[b] = [t_mm]
                    stp = s_tmp[:, 512 * par:512 * (par + 1)]
                    mk.evac_wait("scalar", gu_reg[0][par])
                    W_("scalar", stp_free[par])
                    t_s = D_("scalar", lambda E, stp=stp, pg=pg: E.activation(
                        stp, pg, ACT.Silu), "se")
                    mk.evac_done(gu_reg[0][par], t_s)
                    mk.evac_wait("vector", gu_reg[1][par])
                    W_("vector", t_s)
                    h_done = D_("vector", lambda E, hc=hc, tc=tc, stp=stp, pu=pu: E.tensor_tensor(
                        hh[:, hc, 512 * tc:512 * (tc + 1)], stp, pu, ALU.mult), "ve")
                    mk.evac_done(gu_reg[1][par], h_done)
                    stp_free[par] = h_done
            for oc in range(NTC):
                b = oc % 2
                W_("gpsimd", wd_free[b])
                t_d = None
                for q2 in range(2):
                    t_d = D_("gpsimd", lambda E, ex=ex, oc=oc, b=b, q2=q2: E.dma_start(
                        out=wd_b[b].rearrange("p (hh n) -> p hh n", hh=8)[:, 4 * q2:4 * q2 + 4, :],
                        in_=srcd_e[ex, :, 4 * q2:4 * q2 + 4, 512 * oc:512 * (oc + 1)]), "wt", 16)
                for tt in range(NT):
                    par = (oc * NT + tt) % 2
                    rr = dn_reg[par]
                    pd = dn_dst[par]
                    mk.mm_into(rr)
                    W_("tensor", t_d, h_done)
                    t_mm = None
                    for hc in range(8):
                        t_mm = D_("tensor", lambda E, hc=hc, tt=tt, b=b, pd=pd: E.matmul(
                            pd[:, 0:512], hh[:, hc, 128 * tt:128 * (tt + 1)],
                            wd_b[b][:, 512 * hc:512 * (hc + 1)],
                            start=(hc == 0), stop=(hc == 7)),
                            "te" if hc == 7 else None)
                    mk.mm_done(rr, t_mm)
                    if tt == NT - 1:
                        wd_free[b] = [t_mm]
                    stp = s_tmp[:, 512 * par:512 * (par + 1)]
                    mk.evac_wait("scalar", rr)
                    W_("scalar", gate_tok[tt], stp_free[par])
                    t_s = D_("scalar", lambda E, tt=tt, ex=ex, stp=stp, pd=pd: E.activation(
                        stp, pd[:, 0:512], ACT.Identity,
                        scale=s_comb[:, 8 * tt + ex:8 * tt + ex + 1]), "se")
                    mk.evac_done(rr, t_s)
                    W_("vector", t_s)
                    t_v = D_("vector", lambda E, tt=tt, oc=oc, stp=stp: E.tensor_tensor(
                        acc[:, tt, 512 * oc:512 * (oc + 1)], stp,
                        acc[:, tt, 512 * oc:512 * (oc + 1)], ALU.add), "ve")
                    stp_free[par] = t_v
            h_free = [mk.tok("te")]
        ffn_done_v = mk.tok("ve")
        ffn_done_te = mk.tok("te")

        spill = [None] * NT
        W_("sync", ffn_done_v)
        for tt in range(NT):
            spill[tt] = D_("sync", lambda E, tt=tt: E.dma_start(
                out=d_acc[128 * tt:128 * (tt + 1), :], in_=acc[:, tt, :]), "st", 16)
        spill_done = mk.tok("st")

        Ksb = A_big[:, 0:8192].rearrange("p (kv t) -> p kv t", kv=KV)
        Vsb = A_big[:, 8192:16384].rearrange("p (t n) -> p t n", t=NT)
        Qsb = A_big[:, 16384:24576].rearrange("p (h t) -> p h t", h=H)
        AVs = A_big[:, 24576:32768].rearrange("p (q n) -> p q n", q=4)
        wk_sb = A_s1[:, 0:8192].rearrange("p (cc n) -> p cc n", cc=NCC)
        wv_sb = A_s1[:, 8192:16384].rearrange("p (cc n) -> p cc n", cc=NCC)
        W_("gpsimd", ffn_done_te)
        t_wk = t_wv = None
        srck = d_wk.rearrange("(cc p) n -> p cc n", p=128)
        srcv = d_wv.rearrange("(cc p) n -> p cc n", p=128)
        for q4 in range(4):
            t_wk = D_("gpsimd", lambda E, q4=q4: E.dma_start(
                out=wk_sb[:, 4 * q4:4 * q4 + 4, :],
                in_=srck[:, 4 * q4:4 * q4 + 4, :]), "wt", 16)
            t_wv = D_("gpsimd", lambda E, q4=q4: E.dma_start(
                out=wv_sb[:, 4 * q4:4 * q4 + 4, :],
                in_=srcv[:, 4 * q4:4 * q4 + 4, :]), "wt", 16)

        wq_b = [A_s2[:, 0:2048], A_s2[:, 2048:4096]]
        qraw = [A_s2[:, 4096:4608], A_s2[:, 4608:5120]]
        qswp = [A_s2[:, 5120:5632], A_s2[:, 5632:6144]]
        p_un3 = [A_s2[:, 6144:6784], A_s2[:, 10496:11136], A_s2[:, 11136:11776]]
        pT_sb = A_s2[:, 6784:7424]
        accj = [A_s2[:, 7424:7936], A_s2[:, 7936:8448]]
        out_b = [s2f[:, 4224:4736], s2f[:, 4736:5248]]
        rtA = [A_s2[:, 13184:13696], A_s2[:, 13696:14208]]
        rtB = [A_s2[:, 14208:14720], A_s2[:, 14720:15232]]
        qraw_free = [None, None]
        qswp_free = [None, None]
        rt_free = [None, None]
        kq_reg = ["ps1a", "ps1b", "ps2a", "ps2b"]
        kq_dst = [PS1[:, 0:512], PS1[:, 512:1024], PS2[:, 0:512], PS2[:, 512:1024]]
        rot_i = [0]

        def rope_project(dst_ap, wslice, t0, extra):
            par = rot_i[0] % 4
            rb = rot_i[0] % 2
            rot_i[0] += 1
            rg = [kq_reg[par]]
            pq = kq_dst[par]
            mk.mm_into(rg)
            W_("tensor", spill_done, xn_done, extra)
            t_mm = None
            for cc in range(NCC):
                t_mm = D_("tensor", lambda E, cc=cc, pq=pq, t0=t0: E.matmul(
                    pq, wslice(cc), xn[:, cc, t0:t0 + 512],
                    start=(cc == 0), stop=(cc == NCC - 1)),
                    "te" if cc == NCC - 1 else None)
            mk.mm_done(rg, t_mm)
            mk.evac_wait("scalar", rg)
            W_("scalar", qraw_free[rb])
            t_s = D_("scalar", lambda E, rb=rb, pq=pq: E.copy(qraw[rb], pq), "se")
            mk.evac_done(rg, t_s)
            W_("sync", t_s, qswp_free[rb])
            D_("sync", lambda E, rb=rb: E.dma_start(
                out=qswp[rb][0:64, :], in_=qraw[rb][64:128, :]), "st", 16)
            t_sw = D_("sync", lambda E, rb=rb: E.dma_start(
                out=qswp[rb][64:128, :], in_=qraw[rb][0:64, :]), "st", 16)
            W_("vector", t_s, rt_free[rb], t_ctab, t_stab)
            t_a = D_("vector", lambda E, rb=rb, t0=t0: E.tensor_tensor(
                rtA[rb], qraw[rb], s_ctab[:, t0:t0 + 512], ALU.mult), "ve")
            qraw_free[rb] = t_a
            W_("vector", t_sw)
            t_b2 = D_("vector", lambda E, rb=rb, t0=t0: E.tensor_tensor(
                rtB[rb], qswp[rb], s_stab[:, t0:t0 + 512], ALU.mult), "ve")
            qswp_free[rb] = t_b2
            W_("vector", t_a, t_b2)
            t_o = D_("vector", lambda E, rb=rb, dst=dst_ap: E.tensor_tensor(
                dst, rtA[rb], rtB[rb], ALU.add), "ve")
            rt_free[rb] = t_o
            return t_o

        k_tok = [None] * NT
        v_tok = [None] * NT
        for tc in range(NTC):
            t_o = None
            for kv in range(KV):
                t_o = rope_project(
                    Ksb[:, kv, 512 * tc:512 * (tc + 1)],
                    (lambda cc, kv=kv: wk_sb[:, cc, 128 * kv:128 * (kv + 1)]),
                    512 * tc, t_wk)
            for j in range(4):
                k_tok[4 * tc + j] = t_o
        vrot = [0]
        for tt in range(NT):
            par = vrot[0] % 2
            vrot[0] += 1
            rr = dn_reg[par]
            pv = dn_dst[par]
            mk.mm_into(rr)
            W_("tensor", spill_done, t_wv, xn_done)
            t_mm = None
            for cc in range(NCC):
                t_mm = D_("tensor", lambda E, cc=cc, tt=tt, pv=pv: E.matmul(
                    pv[:, 0:512], xn[:, cc, 128 * tt:128 * (tt + 1)], wv_sb[:, cc, :],
                    start=(cc == 0), stop=(cc == NCC - 1)),
                    "te" if cc == NCC - 1 else None)
            mk.mm_done(rr, t_mm)
            mk.evac_wait("scalar", rr)
            t_s = D_("scalar", lambda E, tt=tt, pv=pv: E.copy(Vsb[:, tt, :], pv[:, 0:512]), "se")
            mk.evac_done(rr, t_s)
            v_tok[tt] = t_s

        wo_sb = [A_s1[:, 0:8192].rearrange("p (hd n) -> p hd n", hd=NCC),
                 A_s1[:, 8192:16384].rearrange("p (hd n) -> p hd n", hd=NCC)]
        kv_mm_done = mk.tok("te")
        wo_free = [[kv_mm_done], [kv_mm_done]]
        wq_free = [[ffn_done_te], [ffn_done_te]]
        accj_free = [None, None]
        out_free = [None, None]
        pun_free = [None, None, None]
        pT_free = [None]
        AVs8 = A_big[:, 16384:32768].rearrange("p (s n) -> p s n", s=8)
        av_ready = [None] * 8
        av_slot_free = [spill_done] * 8
        nmax_free = [None, None, None]
        sc_reg2 = [["ps1a", "ps1b"], ["ps2a", "ps2b"]]
        sc_dst2 = [PS1, PS2]
        qa_free = spill_done
        srcq = d_wq.rearrange("(cc p) n -> p cc n", p=128)
        srco = d_wo.rearrange("(hd p) n -> p hd n", p=128)
        qstage = [A_s2[:, 11776:12288], A_s2[:, 12288:12800]]
        qstage_free = [None, None]
        qjit = [A_s2[:, 15232 + 128 * i:15232 + 128 * (i + 1)] for i in range(4)]
        qjit_free = [[spill_done] for _ in range(4)]
        q_st_tok = [None] * H

        o_jobs = []

        def o_load(job):
            if job["t_wo"] is not None:
                return
            oc, wb = job["oc"], job["wb"]
            W_("gpsimd", wo_free[wb])
            t_wo = None
            for q4 in range(4):
                t_wo = D_("gpsimd", lambda E, oc=oc, wb=wb, q4=q4: E.dma_start(
                    out=wo_sb[wb][:, 4 * q4:4 * q4 + 4, :],
                    in_=srco[:, 4 * q4:4 * q4 + 4, 512 * oc:512 * (oc + 1)]), "wt", 16)
            job["t_wo"] = t_wo

        def o_create(tcs):
            avp = (tcs % 2) * 4
            for oc in range(NTC):
                o_jobs.append(dict(tcs=tcs, oc=oc, wb=oc % 2, t_wo=None, step=0,
                                   avp=avp, t_acc=None))
            o_load(o_jobs[0])
            if len(o_jobs) > 1:
                o_load(o_jobs[1])

        def o_pump():
            if not o_jobs:
                return
            job = o_jobs[0]
            o_load(job)
            if job["step"] == 0 and len(o_jobs) > 2:
                o_load(o_jobs[2])
            st_ = job["step"]
            job["step"] += 1
            if job["step"] == 16:
                o_jobs.pop(0)
            oc, wb, tcs, avp = job["oc"], job["wb"], job["tcs"], job["avp"]
            jjp = st_ // 4
            part = st_ % 4
            qi = 4 * tcs + jjp
            slot = avp + jjp
            par = (oc * 4 + jjp) % 2
            rr = dn_reg[par]
            po = dn_dst[par]
            if part == 0:
                W_("sync", accj_free[par], spill[qi])
                job["t_acc"] = D_("sync", lambda E, qi=qi, oc=oc, par=par: E.dma_start(
                    out=accj[par], in_=d_acc[128 * qi:128 * (qi + 1),
                                             512 * oc:512 * (oc + 1)]), "ld", 16)
                mk.mm_into(rr)
                W_("tensor", av_ready[slot], job["t_wo"])
            t_mm = None
            for h in range(4 * part, 4 * part + 4):
                t_mm = D_("tensor", lambda E, h=h, slot=slot, wb=wb, po=po: E.matmul(
                    po[:, 0:512], AVs8[:, slot, 128 * h:128 * (h + 1)],
                    wo_sb[wb][:, h, :],
                    start=(h == 0), stop=(h == H - 1)),
                    "te" if h == H - 1 else None)
            if part == 3:
                mk.mm_done(rr, t_mm)
                if jjp == 3:
                    wo_free[wb] = [t_mm]
                if oc == NTC - 1:
                    av_slot_free[slot] = t_mm
                mk.evac_wait("vector", rr)
                W_("vector", job["t_acc"], out_free[par])
                t_v = D_("vector", lambda E, par=par, po=po: E.tensor_tensor(
                    out_b[par], po[:, 0:512], accj[par], ALU.add), "ve")
                mk.evac_done(rr, t_v)
                accj_free[par] = t_v
                W_("sync", t_v)
                t_st = D_("sync", lambda E, qi=qi, oc=oc, par=par: E.dma_start(
                    out=d_out[128 * qi:128 * (qi + 1), 512 * oc:512 * (oc + 1)],
                    in_=out_b[par]), "st", 16)
                out_free[par] = t_st

        for tc in range(NTC):
            for h in range(H):
                b = h % 2
                W_("gpsimd", wq_free[b])
                t_wq = None
                for q4 in range(4):
                    t_wq = D_("gpsimd", lambda E, h=h, q4=q4, b=b: E.dma_start(
                        out=wq_b[b].rearrange("p (cc n) -> p cc n", cc=NCC)[:, 4 * q4:4 * q4 + 4, :],
                        in_=srcq[:, 4 * q4:4 * q4 + 4, 128 * h:128 * (h + 1)]), "wt", 16)
                sb = h % 2
                t_o = rope_project(
                    qstage[sb],
                    (lambda cc, b=b: wq_b[b][:, 128 * cc:128 * (cc + 1)]),
                    512 * tc, [t_wq, qstage_free[sb]])
                W_("sync", t_o)
                t_qst = D_("sync", lambda E, h=h, tc=tc, sb=sb: E.dma_start(
                    out=d_q[h, :, 512 * tc:512 * (tc + 1)], in_=qstage[sb]), "st", 16)
                qstage_free[sb] = t_qst
                q_st_tok[h] = t_qst
                wq_free[b] = [mk.tok("te")]

            qld_tok = [None] * (4 * H)
            qld_i = [0]

            def q_load(tc=tc):
                idx = qld_i[0]
                if idx >= 4 * H:
                    return
                qld_i[0] += 1
                jj_, h_ = idx // H, idx % H
                qi_ = 4 * tc + jj_
                qb = idx % 4
                W_("sync", qjit_free[qb], q_st_tok[h_])
                qld_tok[idx] = D_("sync", lambda E, h_=h_, qi_=qi_, qb=qb: E.dma_start(
                    out=qjit[qb], in_=d_q[h_, :, 128 * qi_:128 * (qi_ + 1)]), "ld", 16)

            q_load()
            q_load()
            for jj in range(4):
                qi = 4 * tc + jj
                j0 = max(0, qi - WIN)
                nk = (qi - j0 + 1) * 128
                ntile = nk // 128
                slot = (tc % 2) * 4 + jj
                pend = []

                def stage2(jj=jj, j0=j0, nk=nk, ntile=ntile, qi=qi, slot=slot):
                    h, t_dg = pend.pop(0)
                    sps = h % 3
                    kvh = h // 4
                    pnb = p_un3[sps]
                    dgs = s_dg[:, 128 * sps:128 * (sps + 1)]
                    t_regs = ["pt"] + (["pav"] if ntile == 5 else [])
                    mk.mm_into(t_regs)
                    W_("tensor", t_dg)
                    t_t = None
                    for j in range(ntile):
                        dst = PT[:, 128 * j:128 * (j + 1)] if j < 4 else PAV[:, 128:256]
                        t_t = D_("tensor", lambda E, j=j, pnb=pnb, dgs=dgs, dst=dst: E.matmul(
                            dst, pnb[:, 128 * j:128 * (j + 1)],
                            dgs, start=True, stop=True),
                            "te" if j == ntile - 1 else None)
                    mk.mm_done(t_regs, t_t)
                    pun_free[sps] = t_t
                    mk.evac_wait("scalar", t_regs)
                    W_("scalar", pT_free[0])
                    t_ev = D_("scalar", lambda E, nk=nk: E.copy(
                        pT_sb[:, 0:min(nk, 512)], PT[:, 0:min(nk, 512)]), "se")
                    if ntile == 5:
                        t_ev = D_("scalar", lambda E: E.copy(
                            pT_sb[:, 512:640], PAV[:, 128:256]), "se")
                    mk.evac_done(t_regs, t_ev)
                    mk.mm_into(["pav"])
                    W_("tensor", t_ev, v_tok[qi])
                    t_mm = None
                    for j in range(ntile):
                        t_mm = D_("tensor", lambda E, j=j, j0=j0, kvh=kvh: E.matmul(
                            PAV[:, 0:128], Vsb[:, j0 + j, 128 * kvh:128 * (kvh + 1)],
                            pT_sb[:, 128 * j:128 * (j + 1)],
                            start=(j == 0), stop=(j == ntile - 1)),
                            "te" if j == ntile - 1 else None)
                    mk.mm_done(["pav"], t_mm)
                    pT_free[0] = t_mm
                    mk.evac_wait("vector", ["pav"])
                    W_("vector", av_slot_free[slot])
                    t_e = D_("vector", lambda E, slot=slot, h=h: E.tensor_copy(
                        AVs8[:, slot, 128 * h:128 * (h + 1)], PAV[:, 0:128]), "ve")
                    mk.evac_done(["pav"], t_e)
                    return t_e

                for h in range(H):
                    idx = jj * H + h
                    q_load()
                    spp = h % 2
                    sps = h % 3
                    rg = sc_reg2[spp]
                    ps = sc_dst2[spp]
                    kvh = h // 4
                    qb = idx % 4
                    mk.mm_into(rg)
                    W_("tensor", qld_tok[idx], k_tok[qi], t_masks)
                    nfull = min(nk, 512)
                    D_("tensor", lambda E, ps=ps, qb=qb, kvh=kvh, j0=j0, nfull=nfull: E.matmul(
                        ps[:, 0:nfull], qjit[qb],
                        Ksb[:, kvh, 128 * j0:128 * j0 + nfull],
                        start=True, stop=True), "te")
                    if nk > 512:
                        D_("tensor", lambda E, ps=ps, qb=qb, kvh=kvh, j0=j0, nk=nk: E.matmul(
                            ps[:, 512:nk], qjit[qb],
                            Ksb[:, kvh, 128 * j0 + 512:128 * j0 + nk],
                            start=True, stop=True), "te")
                    t_mm = D_("tensor", lambda E, ps=ps, nk=nk: E.matmul(
                        ps[:, nk - 128:nk], s_ident[:], s_masks[:, 0:128],
                        start=False, stop=True), "te")
                    if qi - j0 == WIN:
                        t_mm = D_("tensor", lambda E, ps=ps: E.matmul(
                            ps[:, 0:128], s_ident[:], s_masks[:, 128:256],
                            start=False, stop=True), "te")
                    mk.mm_done(rg, t_mm)
                    qjit_free[qb] = [t_mm]
                    mk.evac_wait("vector", rg)
                    W_("vector", nmax_free[sps])
                    t_v = D_("vector", lambda E, ps=ps, nk=nk, sps=sps: E.reduce_max(
                        s_nmax[:, sps:sps + 1], ps[:, nk - 128:nk], axis=AX, negate=True), "ve")
                    W_("scalar", t_v, pun_free[sps])
                    t_exp = D_("scalar", lambda E, ps=ps, nk=nk, sps=sps: E.activation(
                        p_un3[sps][:, 0:nk], ps[:, 0:nk], ACT.Exp,
                        bias=s_nmax[:, sps:sps + 1], accum_out=s_sum[:, sps:sps + 1]), "se")
                    mk.evac_done(rg, t_exp)
                    nmax_free[sps] = t_exp
                    W_("vector", t_exp)
                    t_inv = D_("vector", lambda E, sps=sps: E.reciprocal(
                        s_inv[:, sps:sps + 1], s_sum[:, sps:sps + 1]), "ve")
                    W_("vector", t_inv, pun_free[sps])
                    t_dg = D_("vector", lambda E, sps=sps: E.tensor_scalar(
                        s_dg[:, 128 * sps:128 * (sps + 1)], s_ident[:],
                        s_inv[:, sps:sps + 1], None, ALU.mult), "ve")
                    pend.append((h, t_dg))
                    o_pump()
                    if len(pend) >= 2:
                        stage2()
                while pend:
                    stage2()
                av_ready[slot] = mk.tok("se")
                av_ready_v = mk.tok("ve")
                av_ready[slot] = [av_ready[slot], av_ready_v]
            qa_free = mk.tok("te")
            o_create(tc)

        while o_jobs:
            o_pump()

        D_("sync", lambda E, v=mk.cnt["st"]: E.wait_ge(mk.semh["st"], v))

        with nc.Block() as block:
            @block.sync
            def _(E):
                for f in mk.ops["sync"]:
                    f(E)

            @block.gpsimd
            def _(E):
                for f in mk.ops["gpsimd"]:
                    f(E)

            @block.tensor
            def _(E):
                for f in mk.ops["tensor"]:
                    f(E)

            @block.vector
            def _(E):
                for f in mk.ops["vector"]:
                    f(E)

            @block.scalar
            def _(E):
                for f in mk.ops["scalar"]:
                    f(E)

    return nc


def _consts():
    inv = 10000.0 ** (-np.arange(0, 128, 2, dtype=np.float64) / 128.0)
    ang = np.arange(T, dtype=np.float64)[:, None] * inv[None, :]
    s4 = 128.0 ** -0.25
    cosd = np.cos(ang).T * s4
    sind = np.sin(ang).T * s4
    ctab = np.concatenate([cosd, cosd], 0).astype(ml_dtypes.bfloat16)
    stab = np.concatenate([-sind, sind], 0).astype(ml_dtypes.bfloat16)
    ident = np.eye(128, dtype=ml_dtypes.bfloat16)
    ii = np.arange(128)
    diag = np.where(ii[:, None] >= ii[None, :], 0.0, -1e9)
    far = np.where(ii[:, None] < ii[None, :], 0.0, -1e9)
    masks = np.concatenate([diag, far], 1).astype(ml_dtypes.bfloat16)
    return ident, ctab, stab, masks


def kernel(**inp):
    f32 = np.float32
    x = np.ascontiguousarray(np.asarray(inp["L0_attn_x"], f32))
    wqT = np.ascontiguousarray(np.asarray(inp["L0_attn_Wq"], f32).T)
    wkT = np.ascontiguousarray(np.asarray(inp["L0_attn_Wk"], f32).T)
    wvT = np.ascontiguousarray(np.asarray(inp["L0_attn_Wv"], f32).T)
    woT = np.ascontiguousarray(np.asarray(inp["L0_attn_Wo"], f32).T)
    gw9 = np.ascontiguousarray(np.concatenate(
        [np.asarray(inp["L0_ffn_gate_w"], f32),
         np.asarray(inp["L0_ffn_se_gate_w"], f32)], 0).T)
    wgT = np.ascontiguousarray(np.transpose(np.asarray(inp["L0_ffn_w_gate"], f32), (0, 2, 1)))
    wuT = np.ascontiguousarray(np.transpose(np.asarray(inp["L0_ffn_w_up"], f32), (0, 2, 1)))
    wdT = np.ascontiguousarray(np.transpose(np.asarray(inp["L0_ffn_w_down"], f32), (0, 2, 1)))
    shgT = np.ascontiguousarray(np.asarray(inp["L0_ffn_sh_gate"], f32).T)
    shuT = np.ascontiguousarray(np.asarray(inp["L0_ffn_sh_up"], f32).T)
    shdT = np.ascontiguousarray(np.asarray(inp["L0_ffn_sh_down"], f32).T)
    ident, ctab, stab, masks = _consts()

    nc = build_program()
    shared = dict(wqT=wqT, wkT=wkT, wvT=wvT, woT=woT, gw9=gw9, wgT=wgT,
                  wuT=wuT, wdT=wdT, shgT=shgT, shuT=shuT, shdT=shdT,
                  ident=ident, ctab=ctab, stab=stab, masks=masks)
    in_maps = [dict(x=np.ascontiguousarray(x[i]), **shared) for i in range(8)]
    res = run_bass_kernel_spmd(nc, in_maps, core_ids=list(range(8)))
    return np.stack([r["out"] for r in res.results], 0)


# version 22 (final)
# speedup vs baseline: 1.0094x; 1.0094x over previous
import numpy as np
import ml_dtypes
import concourse.bass as bass
import concourse.mybir as mybir
from concourse.bass_utils import run_bass_kernel_spmd
from contextlib import ExitStack

F32 = mybir.dt.float32
BF16 = mybir.dt.bfloat16
AX = mybir.AxisListType.X
ALU = mybir.AluOpType
ACT = mybir.ActivationFunctionType

T = 2048
C = 2048
NT = 16
NCC = 16
NTC = 4
H = 16
KV = 4
D = 128
NE = 8
NHC = 8
WIN = 4
EPS = 1e-5

ENGS = ("sync", "gpsimd", "tensor", "vector", "scalar")


DMA_SEMS = ("ld", "wt", "st")


class MK:

    def __init__(self, used=None):
        self.pass2 = used is not None
        self.used = used if used is not None else set()
        self.nc = bass.Bass("TRN2", target_bir_lowering=False)
        self.ops = {k: [] for k in ENGS}
        self.cnt = {s: 0 for s in ("ld", "wt", "st", "te", "ve", "se")}
        self.seq = {s: 0 for s in ("ld", "wt", "st", "te", "ve", "se")}
        self.val = {}
        self.semh = {}
        self.lastwait = {}
        self.pguard = {}
        self.pprod = {}
        for s in ("ld", "wt", "st", "te", "ve", "se"):
            self.val[(s, 0)] = 0

    def w(self, eng, *tokens):
        flat = []
        for t in tokens:
            if t is None:
                continue
            if isinstance(t, list):
                flat += [u for u in t if u is not None]
            else:
                flat.append(t)
        for tokn in flat:
            s, q = tokn
            if q <= 0:
                continue
            self.used.add((s, q))
            v = self.val[(s, q)]
            if v <= 0:
                continue
            assert v <= self.cnt[s], f"future token {s}:{q} v={v} > {self.cnt[s]}"
            key = (eng, s)
            if self.lastwait.get(key, -1) >= v:
                continue
            self.lastwait[key] = v
            self.ops[eng].append(lambda E, s=s, v=v: E.wait_ge(self.semh[s], v))

    def do(self, eng, make, sem=None, k=1):
        if sem is None:
            self.ops[eng].append(lambda E, m=make: m(E))
            return None
        self.seq[sem] += 1
        q = self.seq[sem]
        need_inc = (not self.pass2) or (sem in DMA_SEMS) or ((sem, q) in self.used)
        if need_inc:
            self.cnt[sem] += k

            def f(E, m=make, s=sem, kk=k):
                m(E).then_inc(self.semh[s], kk)
            self.ops[eng].append(f)
        else:
            self.ops[eng].append(lambda E, m=make: m(E))
        self.val[(sem, q)] = self.cnt[sem]
        return (sem, q)

    def tok(self, sem):
        q = self.seq[sem]
        self.val.setdefault((sem, q), self.cnt[sem])
        return (sem, q)

    def mm_into(self, regions):
        self.w("tensor", *[self.pguard.get(r) for r in regions])

    def mm_done(self, regions, tok):
        for r in regions:
            self.pprod[r] = tok

    def evac_wait(self, eng, regions):
        self.w(eng, *[self.pprod.get(r) for r in regions])

    def evac_done(self, regions, tok):
        for r in regions:
            self.pguard[r] = tok


def build_program():
    mk1 = MK()
    _build(mk1, emit=False)
    mk2 = MK(used=mk1.used)
    return _build(mk2, emit=True)


def _build(mk, emit):
    nc = mk.nc

    d_x = nc.dram_tensor("x", [T, C], F32, kind="ExternalInput").ap()
    d_wq = nc.dram_tensor("wqT", [C, H * D], F32, kind="ExternalInput").ap()
    d_wk = nc.dram_tensor("wkT", [C, KV * D], F32, kind="ExternalInput").ap()
    d_wv = nc.dram_tensor("wvT", [C, KV * D], F32, kind="ExternalInput").ap()
    d_wo = nc.dram_tensor("woT", [H * D, C], F32, kind="ExternalInput").ap()
    d_gw = nc.dram_tensor("gw9", [C, 9], F32, kind="ExternalInput").ap()
    d_wg = nc.dram_tensor("wgT", [NE, C, 1024], F32, kind="ExternalInput").ap()
    d_wu = nc.dram_tensor("wuT", [NE, C, 1024], F32, kind="ExternalInput").ap()
    d_wd = nc.dram_tensor("wdT", [NE, 1024, C], F32, kind="ExternalInput").ap()
    d_shg = nc.dram_tensor("shgT", [C, 2048], F32, kind="ExternalInput").ap()
    d_shu = nc.dram_tensor("shuT", [C, 2048], F32, kind="ExternalInput").ap()
    d_shd = nc.dram_tensor("shdT", [2048, C], F32, kind="ExternalInput").ap()
    d_ident = nc.dram_tensor("ident", [128, 128], BF16, kind="ExternalInput").ap()
    d_ctab = nc.dram_tensor("ctab", [128, T], BF16, kind="ExternalInput").ap()
    d_stab = nc.dram_tensor("stab", [128, T], BF16, kind="ExternalInput").ap()
    d_masks = nc.dram_tensor("masks", [128, 256], BF16, kind="ExternalInput").ap()
    d_out = nc.dram_tensor("out", [T, C], F32, kind="ExternalOutput").ap()
    d_acc = nc.dram_tensor("acc_spill", [T, C], BF16, kind="Internal").ap()
    d_q = nc.dram_tensor("q_scr", [H, 128, T], BF16, kind="Internal").ap()

    with ExitStack() as ctx:
        e = ctx.enter_context
        A_xn = e(nc.sbuf_tensor("A_xn", [128, NCC * T], BF16))
        A_big = e(nc.sbuf_tensor("A_big", [128, NCC * T], BF16))
        A_s1 = e(nc.sbuf_tensor("A_s1", [128, 16384], BF16))
        A_s2 = e(nc.sbuf_tensor("A_s2", [128, 16384], BF16))
        s_ident = e(nc.sbuf_tensor("s_ident", [128, 128], BF16))
        s_ctab = e(nc.sbuf_tensor("s_ctab", [128, T], BF16))
        s_stab = e(nc.sbuf_tensor("s_stab", [128, T], BF16))
        s_masks = e(nc.sbuf_tensor("s_masks", [128, 256], BF16))
        s_gw = e(nc.sbuf_tensor("s_gw", [128, NCC * 9], BF16))
        s_stats = e(nc.sbuf_tensor("s_stats", [128, 8], F32))
        s_sc9 = e(nc.sbuf_tensor("s_sc9", [128, 16], F32))
        s_top8 = e(nc.sbuf_tensor("s_top8", [128, 8], F32))
        s_dd = e(nc.sbuf_tensor("s_dd", [128, 2], F32))
        s_ee = e(nc.sbuf_tensor("s_ee", [128, 2], F32))
        s_w12 = e(nc.sbuf_tensor("s_w12", [128, 2], F32))
        s_eq = e(nc.sbuf_tensor("s_eq", [128, 16], F32))
        s_comb = e(nc.sbuf_tensor("s_comb", [128, NT * NE], F32))
        s_sgate = e(nc.sbuf_tensor("s_sgate", [128, NT], F32))
        s_nmax = e(nc.sbuf_tensor("s_nmax", [128, 4], F32))
        s_sum = e(nc.sbuf_tensor("s_sum", [128, 4], F32))
        s_inv = e(nc.sbuf_tensor("s_inv", [128, 4], F32))
        s_tmp = e(nc.sbuf_tensor("s_tmp", [128, 1024], BF16))
        s_dg = e(nc.sbuf_tensor("s_dg", [128, 384], BF16))

        PS1 = e(nc.psum_tensor("PS1", [128, 1024], F32))
        PS2 = e(nc.psum_tensor("PS2", [128, 1024], F32))
        PT = e(nc.psum_tensor("PT", [128, 512], F32))
        PTbf = None
        PAV = e(nc.psum_tensor("PAV", [128, 512], F32))
        PO = e(nc.psum_tensor("PO", [128, 1024], F32))

        for s in ("ld", "wt", "st", "te", "ve", "se"):
            mk.semh[s] = e(nc.semaphore(s))

        xn = A_xn[:].rearrange("p (c t) -> p c t", c=NCC)
        acc = A_big[:].rearrange("p (t o) -> p t o", t=NT)
        gwv = s_gw[:].rearrange("p (c n) -> p c n", c=NCC)
        s2f = A_s2[:].bitcast(F32)
        PS1bf = PS1[:].bitcast(BF16)
        PTbf = PT[:].bitcast(BF16)

        D_ = mk.do
        W_ = mk.w

        t_ident = D_("sync", lambda E: E.dma_start(out=s_ident[:], in_=d_ident), "ld", 16)
        t_ctab = D_("sync", lambda E: E.dma_start(out=s_ctab[:], in_=d_ctab), "ld", 16)
        t_stab = D_("sync", lambda E: E.dma_start(out=s_stab[:], in_=d_stab), "ld", 16)
        t_masks = D_("sync", lambda E: E.dma_start(out=s_masks[:], in_=d_masks), "ld", 16)
        t_gw = None
        for cc in range(NCC):
            t_gw = D_("gpsimd", lambda E, cc=cc: E.dma_start(
                out=gwv[:, cc, :], in_=d_gw[128 * cc:128 * (cc + 1), :]), "wt", 16)

        xbuf = [s2f[:, 0:2048], s2f[:, 2048:4096]]
        sqd = s2f[:, 4096:6144]
        xnt = [A_s2[:, 12288:14336], A_s2[:, 14336:16384]]
        xbuf_free = [None, None]
        xnt_free = [None, None]
        tr_dst = [PTbf[:, 0:512], PS1bf[:, 0:512]]
        tr_reg = [["pt"], ["ps1a"]]
        acc_init = [None] * NT
        gate_tok = [None] * NT
        sc9_free = None

        for tt in range(NT):
            b = tt % 2
            W_("sync", xbuf_free[b])
            t_x = D_("sync", lambda E, tt=tt, b=b: E.dma_start(
                out=xbuf[b], in_=d_x[128 * tt:128 * (tt + 1), :]), "ld", 16)
            acc_init[tt] = D_("gpsimd", lambda E, tt=tt: E.dma_start(
                out=acc[:, tt, :], in_=d_x[128 * tt:128 * (tt + 1), :]), "wt", 16)
            W_("scalar", t_x)
            t_sq = D_("scalar", lambda E, b=b: E.activation(
                sqd, xbuf[b], ACT.Square, accum_out=s_stats[:, 1:2]), "se")
            W_("vector", t_x)
            t_v = D_("vector", lambda E, b=b: E.reduce_sum(
                s_stats[:, 0:1], xbuf[b], axis=AX, negate=True), "ve")
            W_("vector", t_sq, t_v)
            t_v = D_("vector", lambda E: E.tensor_scalar(
                s_stats[:, 2:3], s_stats[:, 0:1], 1.0 / C, None, ALU.mult), "ve")
            W_("vector", t_v)
            t_v = D_("vector", lambda E: E.tensor_tensor(
                s_stats[:, 3:4], s_stats[:, 2:3], s_stats[:, 2:3], ALU.mult), "ve")
            t_v2 = D_("vector", lambda E: E.tensor_scalar(
                s_stats[:, 4:5], s_stats[:, 1:2], 1.0 / C, None, ALU.mult), "ve")
            W_("vector", t_v, t_v2)
            t_v = D_("vector", lambda E: E.tensor_tensor(
                s_stats[:, 4:5], s_stats[:, 4:5], s_stats[:, 3:4], ALU.subtract), "ve")
            W_("vector", t_v)
            t_v = D_("vector", lambda E: E.tensor_scalar_add(
                s_stats[:, 4:5], s_stats[:, 4:5], EPS), "ve")
            W_("scalar", t_v)
            t_s = D_("scalar", lambda E: E.activation(
                s_stats[:, 5:6], s_stats[:, 4:5], ACT.Ln), "se")
            W_("scalar", t_s)
            t_s = D_("scalar", lambda E: E.activation(
                s_stats[:, 6:7], s_stats[:, 5:6], ACT.Exp, scale=-0.5), "se")
            W_("vector", t_s)
            t_v = D_("vector", lambda E: E.tensor_tensor(
                s_stats[:, 7:8], s_stats[:, 2:3], s_stats[:, 6:7], ALU.mult), "ve")
            W_("scalar", t_v, xnt_free[b])
            t_xn = D_("scalar", lambda E, b=b: E.activation(
                xnt[b], xbuf[b], ACT.Identity, bias=s_stats[:, 7:8],
                scale=s_stats[:, 6:7]), "se")
            xbuf_free[b] = t_xn
            last_evac = None
            for g in range(4):
                rg = tr_reg[g % 2]
                mk.mm_into(rg)
                W_("tensor", t_xn, t_ident)
                t_t = None
                for j in range(4):
                    cc = 4 * g + j
                    t_t = D_("tensor", lambda E, b=b, cc=cc, g=g, j=j: E.transpose(
                        tr_dst[g % 2][:, 128 * j:128 * (j + 1)],
                        xnt[b][:, 128 * cc:128 * (cc + 1)], s_ident[:]),
                        "te" if j == 3 else None)
                mk.mm_done(rg, t_t)
                mk.evac_wait("vector", rg)
                t_ev = D_("vector", lambda E, g=g, tt=tt: E.tensor_copy(
                    xn[:, 4 * g:4 * g + 4, 128 * tt:128 * (tt + 1)],
                    tr_dst[g % 2][:].rearrange("p (c t) -> p c t", c=4)), "ve")
                mk.evac_done(rg, t_ev)
                last_evac = t_ev
            xnt_free[b] = mk.tok("te")
            mk.mm_into(["pav"])
            W_("tensor", last_evac, t_gw)
            t_mm = None
            for cc in range(NCC):
                t_mm = D_("tensor", lambda E, cc=cc, tt=tt: E.matmul(
                    PAV[:, 0:9], xn[:, cc, 128 * tt:128 * (tt + 1)], gwv[:, cc, :],
                    start=(cc == 0), stop=(cc == NCC - 1)),
                    "te" if cc == NCC - 1 else None)
            mk.mm_done(["pav"], t_mm)
            mk.evac_wait("vector", ["pav"])
            W_("vector", sc9_free)
            t_cp = D_("vector", lambda E: E.tensor_copy(s_sc9[:, 0:9], PAV[:, 0:9]), "ve")
            mk.evac_done(["pav"], t_cp)
            W_("vector", t_cp)
            t_m8 = D_("vector", lambda E: E.max(out=s_top8[:], in_=s_sc9[:, 0:8]), "ve")
            W_("vector", t_m8)
            t_d = D_("vector", lambda E: E.tensor_tensor(
                s_dd[:, 0:1], s_top8[:, 0:1], s_top8[:, 1:2], ALU.subtract), "ve")
            W_("vector", t_d)
            t_d2 = D_("vector", lambda E: E.tensor_scalar(
                s_dd[:, 1:2], s_dd[:, 0:1], -1.0, None, ALU.mult), "ve")
            W_("scalar", t_m8, t_d2)
            t_se1 = D_("scalar", lambda E: E.activation(
                s_ee[:, 0:2], s_dd[:, 0:2], ACT.Exp, scale=-1.0), "se")
            t_se2 = D_("scalar", lambda E: E.activation(
                s_sc9[:, 9:10], s_sc9[:, 8:9], ACT.Exp, scale=-1.0), "se")
            sc9_free = t_se2
            W_("vector", t_se1, t_se2)
            t_v = D_("vector", lambda E: E.tensor_scalar_add(
                s_ee[:, 0:2], s_ee[:, 0:2], 1.0), "ve")
            t_v2 = D_("vector", lambda E: E.tensor_scalar_add(
                s_sc9[:, 9:10], s_sc9[:, 9:10], 1.0), "ve")
            W_("vector", t_v, t_v2)
            t_w = D_("vector", lambda E: E.reciprocal(s_w12[:, 0:2], s_ee[:, 0:2]), "ve")
            t_sg = D_("vector", lambda E, tt=tt: E.reciprocal(
                s_sgate[:, tt:tt + 1], s_sc9[:, 9:10]), "ve")
            W_("vector", t_w, t_sg)
            t_e1 = D_("vector", lambda E: E.tensor_single_scalar(
                s_eq[:, 0:8], s_sc9[:, 0:8], s_top8[:, 0:1], ALU.is_equal), "ve")
            t_e2 = D_("vector", lambda E: E.tensor_single_scalar(
                s_eq[:, 8:16], s_sc9[:, 0:8], s_top8[:, 1:2], ALU.is_equal), "ve")
            W_("vector", t_e1, t_e2)
            t_v = D_("vector", lambda E, tt=tt: E.tensor_scalar(
                s_comb[:, 8 * tt:8 * tt + 8], s_eq[:, 0:8], s_w12[:, 0:1], None,
                ALU.mult), "ve")
            W_("vector", t_v)
            gate_tok[tt] = D_("vector", lambda E, tt=tt: E.scalar_tensor_tensor(
                s_comb[:, 8 * tt:8 * tt + 8], s_eq[:, 8:16], s_w12[:, 1:2],
                s_comb[:, 8 * tt:8 * tt + 8], ALU.mult, ALU.add), "ve")

        xn_done = mk.tok("ve")
        p1_done_te = mk.tok("te")
        p1_done_se = mk.tok("se")
        acc_ready = mk.tok("wt")

        shg_b = [A_s2[:, 0:2048], A_s2[:, 2048:4096]]
        shu_b = [A_s2[:, 4096:6144], A_s2[:, 6144:8192]]
        shd_b = [A_s2[:, 8192:12288], A_s2[:, 12288:16384]]
        hid = A_s1[:].rearrange("p (h t) -> p h t", h=8)
        gu_reg = [[["ps1a"], ["ps1b"]], [["ps2a"], ["ps2b"]]]
        dn_reg = [["po0"], ["po1"]]
        dn_dst = [PO[:, 0:512], PO[:, 512:1024]]
        shg_free = [[xn_done, p1_done_te, p1_done_se], [xn_done, p1_done_te, p1_done_se]]
        shu_free = [list(shg_free[0]), list(shg_free[1])]
        shd_free = [list(shg_free[0]), list(shg_free[1])]
        stp_free = [None, None]
        hid_wtok = None

        srcg_sh = d_shg.rearrange("(cc p) n -> p cc n", p=128)
        srcu_sh = d_shu.rearrange("(cc p) n -> p cc n", p=128)
        srcd_sh = d_shd.rearrange("(hh p) n -> p hh n", p=128)

        for half in range(2):
            for hc in range(8):
                hsc = 8 * half + hc
                b = hc % 2
                W_("gpsimd", shg_free[b], shu_free[b])
                t_g = t_u = None
                for q4 in range(4):
                    t_g = D_("gpsimd", lambda E, hsc=hsc, q4=q4, b=b: E.dma_start(
                        out=shg_b[b].rearrange("p (cc n) -> p cc n", cc=NCC)[:, 4 * q4:4 * q4 + 4, :],
                        in_=srcg_sh[:, 4 * q4:4 * q4 + 4, 128 * hsc:128 * (hsc + 1)]), "wt", 16)
                    t_u = D_("gpsimd", lambda E, hsc=hsc, q4=q4, b=b: E.dma_start(
                        out=shu_b[b].rearrange("p (cc n) -> p cc n", cc=NCC)[:, 4 * q4:4 * q4 + 4, :],
                        in_=srcu_sh[:, 4 * q4:4 * q4 + 4, 128 * hsc:128 * (hsc + 1)]), "wt", 16)
                for tc in range(NTC):
                    par = (hc * NTC + tc) % 2
                    rgu = gu_reg[0][par] + gu_reg[1][par]
                    pg = PS1[:, 512 * par:512 * (par + 1)]
                    pu = PS2[:, 512 * par:512 * (par + 1)]
                    mk.mm_into(rgu)
                    W_("tensor", t_g, t_u, xn_done)
                    t_mm = None
                    for cc in range(NCC):
                        D_("tensor", lambda E, cc=cc, tc=tc, b=b, pg=pg: E.matmul(
                            pg, shg_b[b][:, 128 * cc:128 * (cc + 1)],
                            xn[:, cc, 512 * tc:512 * (tc + 1)],
                            start=(cc == 0), stop=(cc == NCC - 1)))
                    for cc in range(NCC):
                        t_mm = D_("tensor", lambda E, cc=cc, tc=tc, b=b, pu=pu: E.matmul(
                            pu, shu_b[b][:, 128 * cc:128 * (cc + 1)],
                            xn[:, cc, 512 * tc:512 * (tc + 1)],
                            start=(cc == 0), stop=(cc == NCC - 1)),
                            "te" if cc == NCC - 1 else None)
                    mk.mm_done(rgu, t_mm)
                    if tc == NTC - 1:
                        shg_free[b] = [t_mm]
                        shu_free[b] = [t_mm]
                    stp = s_tmp[:, 512 * par:512 * (par + 1)]
                    mk.evac_wait("scalar", gu_reg[0][par])
                    W_("scalar", stp_free[par])
                    t_s = D_("scalar", lambda E, stp=stp, pg=pg: E.activation(
                        stp, pg, ACT.Silu), "se")
                    mk.evac_done(gu_reg[0][par], t_s)
                    mk.evac_wait("vector", gu_reg[1][par])
                    W_("vector", t_s)
                    hid_wtok = D_("vector", lambda E, hc=hc, tc=tc, stp=stp, pu=pu: E.tensor_tensor(
                        hid[:, hc, 512 * tc:512 * (tc + 1)], stp, pu, ALU.mult), "ve")
                    mk.evac_done(gu_reg[1][par], hid_wtok)
                    stp_free[par] = hid_wtok
            hid_done = hid_wtok
            for oc in range(NTC):
                b = oc % 2
                W_("gpsimd", shd_free[b])
                t_d = None
                for q2 in range(2):
                    t_d = D_("gpsimd", lambda E, half=half, oc=oc, b=b, q2=q2: E.dma_start(
                        out=shd_b[b].rearrange("p (hh n) -> p hh n", hh=8)[:, 4 * q2:4 * q2 + 4, :],
                        in_=srcd_sh[:, 8 * half + 4 * q2:8 * half + 4 * q2 + 4,
                                    512 * oc:512 * (oc + 1)]), "wt", 16)
                for tt in range(NT):
                    par = (oc * NT + tt) % 2
                    rr = dn_reg[par]
                    pd = dn_dst[par]
                    mk.mm_into(rr)
                    W_("tensor", t_d, hid_done)
                    t_mm = None
                    for hc in range(8):
                        t_mm = D_("tensor", lambda E, hc=hc, tt=tt, b=b, pd=pd: E.matmul(
                            pd[:, 0:512], hid[:, hc, 128 * tt:128 * (tt + 1)],
                            shd_b[b][:, 512 * hc:512 * (hc + 1)],
                            start=(hc == 0), stop=(hc == 7)),
                            "te" if hc == 7 else None)
                    mk.mm_done(rr, t_mm)
                    if tt == NT - 1:
                        shd_free[b] = [t_mm]
                    mk.evac_wait("vector", rr)
                    W_("vector", gate_tok[tt], acc_ready)
                    t_v = D_("vector", lambda E, tt=tt, oc=oc, pd=pd: E.scalar_tensor_tensor(
                        acc[:, tt, 512 * oc:512 * (oc + 1)], pd[:, 0:512],
                        s_sgate[:, tt:tt + 1], acc[:, tt, 512 * oc:512 * (oc + 1)],
                        ALU.mult, ALU.add), "ve")
                    mk.evac_done(rr, t_v)
        shared_done_te = mk.tok("te")

        wg_b = shg_b
        wu_b = shu_b
        wd_b = shd_b
        hh = hid
        wg_free = [[shared_done_te], [shared_done_te]]
        wu_free = [[shared_done_te], [shared_done_te]]
        wd_free = [[shared_done_te], [shared_done_te]]
        h_free = [shared_done_te]
        h_done = None
        srcg_e = d_wg.rearrange("e (cc p) n -> e p cc n", p=128)
        srcu_e = d_wu.rearrange("e (cc p) n -> e p cc n", p=128)
        srcd_e = d_wd.rearrange("e (hh p) n -> e p hh n", p=128)

        for ex in range(NE):
            for hc in range(NHC):
                b = hc % 2
                W_("gpsimd", wg_free[b], wu_free[b])
                t_g = t_u = None
                for q4 in range(4):
                    t_g = D_("gpsimd", lambda E, ex=ex, hc=hc, q4=q4, b=b: E.dma_start(
                        out=wg_b[b].rearrange("p (cc n) -> p cc n", cc=NCC)[:, 4 * q4:4 * q4 + 4, :],
                        in_=srcg_e[ex, :, 4 * q4:4 * q4 + 4, 128 * hc:128 * (hc + 1)]), "wt", 16)
                    t_u = D_("gpsimd", lambda E, ex=ex, hc=hc, q4=q4, b=b: E.dma_start(
                        out=wu_b[b].rearrange("p (cc n) -> p cc n", cc=NCC)[:, 4 * q4:4 * q4 + 4, :],
                        in_=srcu_e[ex, :, 4 * q4:4 * q4 + 4, 128 * hc:128 * (hc + 1)]), "wt", 16)
                for tc in range(NTC):
                    par = (hc * NTC + tc) % 2
                    rgu = gu_reg[0][par] + gu_reg[1][par]
                    pg = PS1[:, 512 * par:512 * (par + 1)]
                    pu = PS2[:, 512 * par:512 * (par + 1)]
                    mk.mm_into(rgu)
                    W_("tensor", t_g, t_u, h_free)
                    t_mm = None
                    for cc in range(NCC):
                        D_("tensor", lambda E, cc=cc, tc=tc, b=b, pg=pg: E.matmul(
                            pg, wg_b[b][:, 128 * cc:128 * (cc + 1)],
                            xn[:, cc, 512 * tc:512 * (tc + 1)],
                            start=(cc == 0), stop=(cc == NCC - 1)))
                    for cc in range(NCC):
                        t_mm = D_("tensor", lambda E, cc=cc, tc=tc, b=b, pu=pu: E.matmul(
                            pu, wu_b[b][:, 128 * cc:128 * (cc + 1)],
                            xn[:, cc, 512 * tc:512 * (tc + 1)],
                            start=(cc == 0), stop=(cc == NCC - 1)),
                            "te" if cc == NCC - 1 else None)
                    mk.mm_done(rgu, t_mm)
                    if tc == NTC - 1:
                        wg_free[b] = [t_mm]
                        wu_free[b] = [t_mm]
                    stp = s_tmp[:, 512 * par:512 * (par + 1)]
                    mk.evac_wait("scalar", gu_reg[0][par])
                    W_("scalar", stp_free[par])
                    t_s = D_("scalar", lambda E, stp=stp, pg=pg: E.activation(
                        stp, pg, ACT.Silu), "se")
                    mk.evac_done(gu_reg[0][par], t_s)
                    mk.evac_wait("vector", gu_reg[1][par])
                    W_("vector", t_s)
                    h_done = D_("vector", lambda E, hc=hc, tc=tc, stp=stp, pu=pu: E.tensor_tensor(
                        hh[:, hc, 512 * tc:512 * (tc + 1)], stp, pu, ALU.mult), "ve")
                    mk.evac_done(gu_reg[1][par], h_done)
                    stp_free[par] = h_done
            for oc in range(NTC):
                b = oc % 2
                W_("gpsimd", wd_free[b])
                t_d = None
                for q2 in range(2):
                    t_d = D_("gpsimd", lambda E, ex=ex, oc=oc, b=b, q2=q2: E.dma_start(
                        out=wd_b[b].rearrange("p (hh n) -> p hh n", hh=8)[:, 4 * q2:4 * q2 + 4, :],
                        in_=srcd_e[ex, :, 4 * q2:4 * q2 + 4, 512 * oc:512 * (oc + 1)]), "wt", 16)
                for tt in range(NT):
                    par = (oc * NT + tt) % 2
                    rr = dn_reg[par]
                    pd = dn_dst[par]
                    mk.mm_into(rr)
                    W_("tensor", t_d, h_done)
                    t_mm = None
                    for hc in range(8):
                        t_mm = D_("tensor", lambda E, hc=hc, tt=tt, b=b, pd=pd: E.matmul(
                            pd[:, 0:512], hh[:, hc, 128 * tt:128 * (tt + 1)],
                            wd_b[b][:, 512 * hc:512 * (hc + 1)],
                            start=(hc == 0), stop=(hc == 7)),
                            "te" if hc == 7 else None)
                    mk.mm_done(rr, t_mm)
                    if tt == NT - 1:
                        wd_free[b] = [t_mm]
                    stp = s_tmp[:, 512 * par:512 * (par + 1)]
                    mk.evac_wait("scalar", rr)
                    W_("scalar", gate_tok[tt], stp_free[par])
                    t_s = D_("scalar", lambda E, tt=tt, ex=ex, stp=stp, pd=pd: E.activation(
                        stp, pd[:, 0:512], ACT.Identity,
                        scale=s_comb[:, 8 * tt + ex:8 * tt + ex + 1]), "se")
                    mk.evac_done(rr, t_s)
                    W_("vector", t_s)
                    t_v = D_("vector", lambda E, tt=tt, oc=oc, stp=stp: E.tensor_tensor(
                        acc[:, tt, 512 * oc:512 * (oc + 1)], stp,
                        acc[:, tt, 512 * oc:512 * (oc + 1)], ALU.add), "ve")
                    stp_free[par] = t_v
            h_free = [mk.tok("te")]
        ffn_done_v = mk.tok("ve")
        ffn_done_te = mk.tok("te")

        spill = [None] * NT
        W_("sync", ffn_done_v)
        for tt in range(NT):
            spill[tt] = D_("sync", lambda E, tt=tt: E.dma_start(
                out=d_acc[128 * tt:128 * (tt + 1), :], in_=acc[:, tt, :]), "st", 16)
        spill_done = mk.tok("st")

        Ksb = A_big[:, 0:8192].rearrange("p (kv t) -> p kv t", kv=KV)
        Vsb = A_big[:, 8192:16384].rearrange("p (t n) -> p t n", t=NT)
        Qsb = A_big[:, 16384:24576].rearrange("p (h t) -> p h t", h=H)
        AVs = A_big[:, 24576:32768].rearrange("p (q n) -> p q n", q=4)
        wk_sb = A_s1[:, 0:8192].rearrange("p (cc n) -> p cc n", cc=NCC)
        wv_sb = A_s1[:, 8192:16384].rearrange("p (cc n) -> p cc n", cc=NCC)
        W_("gpsimd", ffn_done_te)
        t_wk = t_wv = None
        srck = d_wk.rearrange("(cc p) n -> p cc n", p=128)
        srcv = d_wv.rearrange("(cc p) n -> p cc n", p=128)
        for q4 in range(4):
            t_wk = D_("gpsimd", lambda E, q4=q4: E.dma_start(
                out=wk_sb[:, 4 * q4:4 * q4 + 4, :],
                in_=srck[:, 4 * q4:4 * q4 + 4, :]), "wt", 16)
            t_wv = D_("gpsimd", lambda E, q4=q4: E.dma_start(
                out=wv_sb[:, 4 * q4:4 * q4 + 4, :],
                in_=srcv[:, 4 * q4:4 * q4 + 4, :]), "wt", 16)

        wq_b = [A_s2[:, 0:2048], A_s2[:, 2048:4096]]
        qraw = [A_s2[:, 4096:4608], A_s2[:, 4608:5120]]
        qswp = [A_s2[:, 5120:5632], A_s2[:, 5632:6144]]
        p_un3 = [A_s2[:, 6144:6784], A_s2[:, 10496:11136], A_s2[:, 11136:11776]]
        pT_sb = A_s2[:, 6784:7424]
        accj = [A_s2[:, 7424:7936], A_s2[:, 7936:8448]]
        out_b = [s2f[:, 4224:4736], s2f[:, 4736:5248]]
        rtA = [A_s2[:, 13184:13696], A_s2[:, 13696:14208]]
        rtB = [A_s2[:, 14208:14720], A_s2[:, 14720:15232]]
        qraw_free = [None, None]
        qswp_free = [None, None]
        rt_free = [None, None]
        kq_reg = ["ps1a", "ps1b", "ps2a", "ps2b"]
        kq_dst = [PS1[:, 0:512], PS1[:, 512:1024], PS2[:, 0:512], PS2[:, 512:1024]]
        rot_i = [0]

        def rope_project(dst_ap, wslice, t0, extra):
            par = rot_i[0] % 4
            rb = rot_i[0] % 2
            rot_i[0] += 1
            rg = [kq_reg[par]]
            pq = kq_dst[par]
            mk.mm_into(rg)
            W_("tensor", spill_done, xn_done, extra)
            t_mm = None
            for cc in range(NCC):
                t_mm = D_("tensor", lambda E, cc=cc, pq=pq, t0=t0: E.matmul(
                    pq, wslice(cc), xn[:, cc, t0:t0 + 512],
                    start=(cc == 0), stop=(cc == NCC - 1)),
                    "te" if cc == NCC - 1 else None)
            mk.mm_done(rg, t_mm)
            mk.evac_wait("scalar", rg)
            W_("scalar", qraw_free[rb])
            t_s = D_("scalar", lambda E, rb=rb, pq=pq: E.copy(qraw[rb], pq), "se")
            mk.evac_done(rg, t_s)
            W_("sync", t_s, qswp_free[rb])
            D_("sync", lambda E, rb=rb: E.dma_start(
                out=qswp[rb][0:64, :], in_=qraw[rb][64:128, :]), "st", 16)
            t_sw = D_("sync", lambda E, rb=rb: E.dma_start(
                out=qswp[rb][64:128, :], in_=qraw[rb][0:64, :]), "st", 16)
            W_("vector", t_s, rt_free[rb], t_ctab, t_stab)
            t_a = D_("vector", lambda E, rb=rb, t0=t0: E.tensor_tensor(
                rtA[rb], qraw[rb], s_ctab[:, t0:t0 + 512], ALU.mult), "ve")
            qraw_free[rb] = t_a
            W_("vector", t_sw)
            t_b2 = D_("vector", lambda E, rb=rb, t0=t0: E.tensor_tensor(
                rtB[rb], qswp[rb], s_stab[:, t0:t0 + 512], ALU.mult), "ve")
            qswp_free[rb] = t_b2
            W_("vector", t_a, t_b2)
            t_o = D_("vector", lambda E, rb=rb, dst=dst_ap: E.tensor_tensor(
                dst, rtA[rb], rtB[rb], ALU.add), "ve")
            rt_free[rb] = t_o
            return t_o

        k_tok = [None] * NT
        v_tok = [None] * NT
        for tc in range(NTC):
            t_o = None
            for kv in range(KV):
                t_o = rope_project(
                    Ksb[:, kv, 512 * tc:512 * (tc + 1)],
                    (lambda cc, kv=kv: wk_sb[:, cc, 128 * kv:128 * (kv + 1)]),
                    512 * tc, t_wk)
            for j in range(4):
                k_tok[4 * tc + j] = t_o
        vrot = [0]
        for tt in range(NT):
            par = vrot[0] % 2
            vrot[0] += 1
            rr = dn_reg[par]
            pv = dn_dst[par]
            mk.mm_into(rr)
            W_("tensor", spill_done, t_wv, xn_done)
            t_mm = None
            for cc in range(NCC):
                t_mm = D_("tensor", lambda E, cc=cc, tt=tt, pv=pv: E.matmul(
                    pv[:, 0:512], xn[:, cc, 128 * tt:128 * (tt + 1)], wv_sb[:, cc, :],
                    start=(cc == 0), stop=(cc == NCC - 1)),
                    "te" if cc == NCC - 1 else None)
            mk.mm_done(rr, t_mm)
            mk.evac_wait("scalar", rr)
            t_s = D_("scalar", lambda E, tt=tt, pv=pv: E.copy(Vsb[:, tt, :], pv[:, 0:512]), "se")
            mk.evac_done(rr, t_s)
            v_tok[tt] = t_s

        wo_sb = [A_s1[:, 0:8192].rearrange("p (hd n) -> p hd n", hd=NCC),
                 A_s1[:, 8192:16384].rearrange("p (hd n) -> p hd n", hd=NCC)]
        kv_mm_done = mk.tok("te")
        wo_free = [[kv_mm_done], [kv_mm_done]]
        wq_free = [[ffn_done_te], [ffn_done_te]]
        accj_free = [None, None]
        out_free = [None, None]
        pun_free = [None, None, None]
        pT_free = [None]
        AVs8 = A_big[:, 16384:32768].rearrange("p (s n) -> p s n", s=8)
        av_ready = [None] * 8
        av_slot_free = [spill_done] * 8
        nmax_free = [None, None, None]
        sc_reg2 = [["ps1a", "ps1b"], ["ps2a", "ps2b"]]
        sc_dst2 = [PS1, PS2]
        qa_free = spill_done
        srcq = d_wq.rearrange("(cc p) n -> p cc n", p=128)
        srco = d_wo.rearrange("(hd p) n -> p hd n", p=128)
        qstage = [A_s2[:, 11776:12288], A_s2[:, 12288:12800]]
        qstage_free = [None, None]
        qjit = [A_s2[:, 15232 + 128 * i:15232 + 128 * (i + 1)] for i in range(4)]
        qjit_free = [[spill_done] for _ in range(4)]
        q_st_tok = [None] * H

        o_jobs = []
        drained = {}

        def o_load(job):
            if job["t_wo"] is not None:
                return
            oc, wb = job["oc"], job["wb"]
            W_("gpsimd", wo_free[wb])
            t_wo = None
            for q4 in range(4):
                t_wo = D_("gpsimd", lambda E, oc=oc, wb=wb, q4=q4: E.dma_start(
                    out=wo_sb[wb][:, 4 * q4:4 * q4 + 4, :],
                    in_=srco[:, 4 * q4:4 * q4 + 4, 512 * oc:512 * (oc + 1)]), "wt", 16)
            job["t_wo"] = t_wo

        def o_create(tcs):
            avp = (tcs % 2) * 4
            for oc in range(NTC):
                o_jobs.append(dict(tcs=tcs, oc=oc, wb=oc % 2, t_wo=None, step=0,
                                   avp=avp, t_acc=None))
            o_load(o_jobs[0])
            if len(o_jobs) > 1:
                o_load(o_jobs[1])

        def o_pump():
            if not o_jobs:
                return
            job = o_jobs[0]
            if job["step"] // 4 > drained.get(job["tcs"], -1):
                return
            o_load(job)
            if job["step"] == 0 and len(o_jobs) > 2:
                o_load(o_jobs[2])
            st_ = job["step"]
            job["step"] += 1
            if job["step"] == 16:
                o_jobs.pop(0)
            oc, wb, tcs, avp = job["oc"], job["wb"], job["tcs"], job["avp"]
            jjp = st_ // 4
            part = st_ % 4
            qi = 4 * tcs + jjp
            slot = avp + jjp
            par = (oc * 4 + jjp) % 2
            rr = dn_reg[par]
            po = dn_dst[par]
            if part == 0:
                W_("sync", accj_free[par], spill[qi])
                job["t_acc"] = D_("sync", lambda E, qi=qi, oc=oc, par=par: E.dma_start(
                    out=accj[par], in_=d_acc[128 * qi:128 * (qi + 1),
                                             512 * oc:512 * (oc + 1)]), "ld", 16)
                mk.mm_into(rr)
                W_("tensor", av_ready[slot], job["t_wo"])
            t_mm = None
            for h in range(4 * part, 4 * part + 4):
                t_mm = D_("tensor", lambda E, h=h, slot=slot, wb=wb, po=po: E.matmul(
                    po[:, 0:512], AVs8[:, slot, 128 * h:128 * (h + 1)],
                    wo_sb[wb][:, h, :],
                    start=(h == 0), stop=(h == H - 1)),
                    "te" if h == H - 1 else None)
            if part == 3:
                mk.mm_done(rr, t_mm)
                if jjp == 3:
                    wo_free[wb] = [t_mm]
                if oc == NTC - 1:
                    av_slot_free[slot] = t_mm
                mk.evac_wait("vector", rr)
                W_("vector", job["t_acc"], out_free[par])
                t_v = D_("vector", lambda E, par=par, po=po: E.tensor_tensor(
                    out_b[par], po[:, 0:512], accj[par], ALU.add), "ve")
                mk.evac_done(rr, t_v)
                accj_free[par] = t_v
                W_("sync", t_v)
                t_st = D_("sync", lambda E, qi=qi, oc=oc, par=par: E.dma_start(
                    out=d_out[128 * qi:128 * (qi + 1), 512 * oc:512 * (oc + 1)],
                    in_=out_b[par]), "st", 16)
                out_free[par] = t_st

        for tc in range(NTC):
            for h in range(H):
                b = h % 2
                W_("gpsimd", wq_free[b])
                t_wq = None
                for q4 in range(4):
                    t_wq = D_("gpsimd", lambda E, h=h, q4=q4, b=b: E.dma_start(
                        out=wq_b[b].rearrange("p (cc n) -> p cc n", cc=NCC)[:, 4 * q4:4 * q4 + 4, :],
                        in_=srcq[:, 4 * q4:4 * q4 + 4, 128 * h:128 * (h + 1)]), "wt", 16)
                sb = h % 2
                t_o = rope_project(
                    qstage[sb],
                    (lambda cc, b=b: wq_b[b][:, 128 * cc:128 * (cc + 1)]),
                    512 * tc, [t_wq, qstage_free[sb]])
                W_("sync", t_o)
                t_qst = D_("sync", lambda E, h=h, tc=tc, sb=sb: E.dma_start(
                    out=d_q[h, :, 512 * tc:512 * (tc + 1)], in_=qstage[sb]), "st", 16)
                qstage_free[sb] = t_qst
                q_st_tok[h] = t_qst
                wq_free[b] = [mk.tok("te")]

            qld_tok = [None] * (4 * H)
            qld_i = [0]

            def q_load(tc=tc):
                idx = qld_i[0]
                if idx >= 4 * H:
                    return
                qld_i[0] += 1
                jj_, h_ = idx // H, idx % H
                qi_ = 4 * tc + jj_
                qb = idx % 4
                W_("sync", qjit_free[qb], q_st_tok[h_])
                qld_tok[idx] = D_("sync", lambda E, h_=h_, qi_=qi_, qb=qb: E.dma_start(
                    out=qjit[qb], in_=d_q[h_, :, 128 * qi_:128 * (qi_ + 1)]), "ld", 16)

            q_load()
            q_load()
            for jj in range(4):
                qi = 4 * tc + jj
                j0 = max(0, qi - WIN)
                nk = (qi - j0 + 1) * 128
                ntile = nk // 128
                slot = (tc % 2) * 4 + jj
                pend = []

                def stage2(jj=jj, j0=j0, nk=nk, ntile=ntile, qi=qi, slot=slot):
                    h, t_dg = pend.pop(0)
                    sps = h % 3
                    kvh = h // 4
                    pnb = p_un3[sps]
                    dgs = s_dg[:, 128 * sps:128 * (sps + 1)]
                    t_regs = ["pt"] + (["pav"] if ntile == 5 else [])
                    mk.mm_into(t_regs)
                    W_("tensor", t_dg)
                    t_t = None
                    for j in range(ntile):
                        dst = PT[:, 128 * j:128 * (j + 1)] if j < 4 else PAV[:, 128:256]
                        t_t = D_("tensor", lambda E, j=j, pnb=pnb, dgs=dgs, dst=dst: E.matmul(
                            dst, pnb[:, 128 * j:128 * (j + 1)],
                            dgs, start=True, stop=True),
                            "te" if j == ntile - 1 else None)
                    mk.mm_done(t_regs, t_t)
                    pun_free[sps] = t_t
                    mk.evac_wait("scalar", t_regs)
                    W_("scalar", pT_free[0])
                    t_ev = D_("scalar", lambda E, nk=nk: E.copy(
                        pT_sb[:, 0:min(nk, 512)], PT[:, 0:min(nk, 512)]), "se")
                    if ntile == 5:
                        t_ev = D_("scalar", lambda E: E.copy(
                            pT_sb[:, 512:640], PAV[:, 128:256]), "se")
                    mk.evac_done(t_regs, t_ev)
                    mk.mm_into(["pav"])
                    W_("tensor", t_ev, v_tok[qi])
                    t_mm = None
                    for j in range(ntile):
                        t_mm = D_("tensor", lambda E, j=j, j0=j0, kvh=kvh: E.matmul(
                            PAV[:, 0:128], Vsb[:, j0 + j, 128 * kvh:128 * (kvh + 1)],
                            pT_sb[:, 128 * j:128 * (j + 1)],
                            start=(j == 0), stop=(j == ntile - 1)),
                            "te" if j == ntile - 1 else None)
                    mk.mm_done(["pav"], t_mm)
                    pT_free[0] = t_mm
                    mk.evac_wait("vector", ["pav"])
                    W_("vector", av_slot_free[slot])
                    t_e = D_("vector", lambda E, slot=slot, h=h: E.tensor_copy(
                        AVs8[:, slot, 128 * h:128 * (h + 1)], PAV[:, 0:128]), "ve")
                    mk.evac_done(["pav"], t_e)
                    return t_e

                for h in range(H):
                    idx = jj * H + h
                    q_load()
                    spp = h % 2
                    sps = h % 3
                    rg = sc_reg2[spp]
                    ps = sc_dst2[spp]
                    kvh = h // 4
                    qb = idx % 4
                    mk.mm_into(rg)
                    W_("tensor", qld_tok[idx], k_tok[qi], t_masks)
                    nfull = min(nk, 512)
                    D_("tensor", lambda E, ps=ps, qb=qb, kvh=kvh, j0=j0, nfull=nfull: E.matmul(
                        ps[:, 0:nfull], qjit[qb],
                        Ksb[:, kvh, 128 * j0:128 * j0 + nfull],
                        start=True, stop=True), "te")
                    if nk > 512:
                        D_("tensor", lambda E, ps=ps, qb=qb, kvh=kvh, j0=j0, nk=nk: E.matmul(
                            ps[:, 512:nk], qjit[qb],
                            Ksb[:, kvh, 128 * j0 + 512:128 * j0 + nk],
                            start=True, stop=True), "te")
                    t_mm = D_("tensor", lambda E, ps=ps, nk=nk: E.matmul(
                        ps[:, nk - 128:nk], s_ident[:], s_masks[:, 0:128],
                        start=False, stop=True), "te")
                    if qi - j0 == WIN:
                        t_mm = D_("tensor", lambda E, ps=ps: E.matmul(
                            ps[:, 0:128], s_ident[:], s_masks[:, 128:256],
                            start=False, stop=True), "te")
                    mk.mm_done(rg, t_mm)
                    qjit_free[qb] = [t_mm]
                    mk.evac_wait("vector", rg)
                    W_("vector", nmax_free[sps])
                    t_v = D_("vector", lambda E, ps=ps, nk=nk, sps=sps: E.reduce_max(
                        s_nmax[:, sps:sps + 1], ps[:, nk - 128:nk], axis=AX, negate=True), "ve")
                    W_("scalar", t_v, pun_free[sps])
                    t_exp = D_("scalar", lambda E, ps=ps, nk=nk, sps=sps: E.activation(
                        p_un3[sps][:, 0:nk], ps[:, 0:nk], ACT.Exp,
                        bias=s_nmax[:, sps:sps + 1], accum_out=s_sum[:, sps:sps + 1]), "se")
                    mk.evac_done(rg, t_exp)
                    nmax_free[sps] = t_exp
                    W_("vector", t_exp)
                    t_inv = D_("vector", lambda E, sps=sps: E.reciprocal(
                        s_inv[:, sps:sps + 1], s_sum[:, sps:sps + 1]), "ve")
                    W_("vector", t_inv, pun_free[sps])
                    t_dg = D_("vector", lambda E, sps=sps: E.tensor_scalar(
                        s_dg[:, 128 * sps:128 * (sps + 1)], s_ident[:],
                        s_inv[:, sps:sps + 1], None, ALU.mult), "ve")
                    pend.append((h, t_dg))
                    o_pump()
                    if len(pend) >= 2:
                        stage2()
                while pend:
                    stage2()
                av_ready[slot] = mk.tok("se")
                av_ready_v = mk.tok("ve")
                av_ready[slot] = [av_ready[slot], av_ready_v]
                drained[tc] = jj
                if jj == 0:
                    o_create(tc)
            qa_free = mk.tok("te")

        while o_jobs:
            o_pump()

        D_("sync", lambda E, v=mk.cnt["st"]: E.wait_ge(mk.semh["st"], v))

        if emit:
            with nc.Block() as block:
                @block.sync
                def _(E):
                    for f in mk.ops["sync"]:
                        f(E)

                @block.gpsimd
                def _(E):
                    for f in mk.ops["gpsimd"]:
                        f(E)

                @block.tensor
                def _(E):
                    for f in mk.ops["tensor"]:
                        f(E)

                @block.vector
                def _(E):
                    for f in mk.ops["vector"]:
                        f(E)

                @block.scalar
                def _(E):
                    for f in mk.ops["scalar"]:
                        f(E)

    return nc


def _consts():
    inv = 10000.0 ** (-np.arange(0, 128, 2, dtype=np.float64) / 128.0)
    ang = np.arange(T, dtype=np.float64)[:, None] * inv[None, :]
    s4 = 128.0 ** -0.25
    cosd = np.cos(ang).T * s4
    sind = np.sin(ang).T * s4
    ctab = np.concatenate([cosd, cosd], 0).astype(ml_dtypes.bfloat16)
    stab = np.concatenate([-sind, sind], 0).astype(ml_dtypes.bfloat16)
    ident = np.eye(128, dtype=ml_dtypes.bfloat16)
    ii = np.arange(128)
    diag = np.where(ii[:, None] >= ii[None, :], 0.0, -1e9)
    far = np.where(ii[:, None] < ii[None, :], 0.0, -1e9)
    masks = np.concatenate([diag, far], 1).astype(ml_dtypes.bfloat16)
    return ident, ctab, stab, masks


def kernel(**inp):
    f32 = np.float32
    x = np.ascontiguousarray(np.asarray(inp["L0_attn_x"], f32))
    wqT = np.ascontiguousarray(np.asarray(inp["L0_attn_Wq"], f32).T)
    wkT = np.ascontiguousarray(np.asarray(inp["L0_attn_Wk"], f32).T)
    wvT = np.ascontiguousarray(np.asarray(inp["L0_attn_Wv"], f32).T)
    woT = np.ascontiguousarray(np.asarray(inp["L0_attn_Wo"], f32).T)
    gw9 = np.ascontiguousarray(np.concatenate(
        [np.asarray(inp["L0_ffn_gate_w"], f32),
         np.asarray(inp["L0_ffn_se_gate_w"], f32)], 0).T)
    wgT = np.ascontiguousarray(np.transpose(np.asarray(inp["L0_ffn_w_gate"], f32), (0, 2, 1)))
    wuT = np.ascontiguousarray(np.transpose(np.asarray(inp["L0_ffn_w_up"], f32), (0, 2, 1)))
    wdT = np.ascontiguousarray(np.transpose(np.asarray(inp["L0_ffn_w_down"], f32), (0, 2, 1)))
    shgT = np.ascontiguousarray(np.asarray(inp["L0_ffn_sh_gate"], f32).T)
    shuT = np.ascontiguousarray(np.asarray(inp["L0_ffn_sh_up"], f32).T)
    shdT = np.ascontiguousarray(np.asarray(inp["L0_ffn_sh_down"], f32).T)
    ident, ctab, stab, masks = _consts()

    nc = build_program()
    shared = dict(wqT=wqT, wkT=wkT, wvT=wvT, woT=woT, gw9=gw9, wgT=wgT,
                  wuT=wuT, wdT=wdT, shgT=shgT, shuT=shuT, shdT=shdT,
                  ident=ident, ctab=ctab, stab=stab, masks=masks)
    in_maps = [dict(x=np.ascontiguousarray(x[i]), **shared) for i in range(8)]
    res = run_bass_kernel_spmd(nc, in_maps, core_ids=list(range(8)))
    return np.stack([r["out"] for r in res.results], 0)


# version 23
# speedup vs baseline: 1.0145x; 1.0051x over previous
import numpy as np
import ml_dtypes
import concourse.bass as bass
import concourse.mybir as mybir
from concourse.bass_utils import run_bass_kernel_spmd
from contextlib import ExitStack

F32 = mybir.dt.float32
BF16 = mybir.dt.bfloat16
AX = mybir.AxisListType.X
ALU = mybir.AluOpType
ACT = mybir.ActivationFunctionType

T = 2048
C = 2048
NT = 16
NCC = 16
NTC = 4
H = 16
KV = 4
D = 128
NE = 8
NHC = 8
WIN = 4
EPS = 1e-5

ENGS = ("sync", "gpsimd", "tensor", "vector", "scalar")


DMA_SEMS = ("ld", "wt", "st")


class MK:

    def __init__(self, used=None):
        self.pass2 = used is not None
        self.used = used if used is not None else set()
        self.nc = bass.Bass("TRN2", target_bir_lowering=False)
        self.ops = {k: [] for k in ENGS}
        self.cnt = {s: 0 for s in ("ld", "wt", "st", "te", "ve", "se")}
        self.seq = {s: 0 for s in ("ld", "wt", "st", "te", "ve", "se")}
        self.val = {}
        self.semh = {}
        self.lastwait = {}
        self.pguard = {}
        self.pprod = {}
        for s in ("ld", "wt", "st", "te", "ve", "se"):
            self.val[(s, 0)] = 0

    def w(self, eng, *tokens):
        flat = []
        for t in tokens:
            if t is None:
                continue
            if isinstance(t, list):
                flat += [u for u in t if u is not None]
            else:
                flat.append(t)
        for tokn in flat:
            s, q = tokn
            if q <= 0:
                continue
            self.used.add((s, q))
            v = self.val[(s, q)]
            if v <= 0:
                continue
            assert v <= self.cnt[s], f"future token {s}:{q} v={v} > {self.cnt[s]}"
            key = (eng, s)
            if self.lastwait.get(key, -1) >= v:
                continue
            self.lastwait[key] = v
            self.ops[eng].append(lambda E, s=s, v=v: E.wait_ge(self.semh[s], v))

    def do(self, eng, make, sem=None, k=1):
        if sem is None:
            self.ops[eng].append(lambda E, m=make: m(E))
            return None
        self.seq[sem] += 1
        q = self.seq[sem]
        need_inc = (not self.pass2) or (sem in DMA_SEMS) or ((sem, q) in self.used)
        if need_inc:
            self.cnt[sem] += k

            def f(E, m=make, s=sem, kk=k):
                m(E).then_inc(self.semh[s], kk)
            self.ops[eng].append(f)
        else:
            self.ops[eng].append(lambda E, m=make: m(E))
        self.val[(sem, q)] = self.cnt[sem]
        return (sem, q)

    def tok(self, sem):
        q = self.seq[sem]
        self.val.setdefault((sem, q), self.cnt[sem])
        return (sem, q)

    def mm_into(self, regions):
        self.w("tensor", *[self.pguard.get(r) for r in regions])

    def mm_done(self, regions, tok):
        for r in regions:
            self.pprod[r] = tok

    def evac_wait(self, eng, regions):
        self.w(eng, *[self.pprod.get(r) for r in regions])

    def evac_done(self, regions, tok):
        for r in regions:
            self.pguard[r] = tok


def build_program():
    mk1 = MK()
    _build(mk1, emit=False)
    mk2 = MK(used=mk1.used)
    return _build(mk2, emit=True)


def _build(mk, emit):
    nc = mk.nc

    d_x = nc.dram_tensor("x", [T, C], F32, kind="ExternalInput").ap()
    d_wq = nc.dram_tensor("wqT", [C, H * D], F32, kind="ExternalInput").ap()
    d_wk = nc.dram_tensor("wkT", [C, KV * D], F32, kind="ExternalInput").ap()
    d_wv = nc.dram_tensor("wvT", [C, KV * D], F32, kind="ExternalInput").ap()
    d_wo = nc.dram_tensor("woT", [H * D, C], F32, kind="ExternalInput").ap()
    d_gw = nc.dram_tensor("gw9", [C, 9], F32, kind="ExternalInput").ap()
    d_wg = nc.dram_tensor("wgT", [NE, C, 1024], F32, kind="ExternalInput").ap()
    d_wu = nc.dram_tensor("wuT", [NE, C, 1024], F32, kind="ExternalInput").ap()
    d_wd = nc.dram_tensor("wdT", [NE, 1024, C], F32, kind="ExternalInput").ap()
    d_shg = nc.dram_tensor("shgT", [C, 2048], F32, kind="ExternalInput").ap()
    d_shu = nc.dram_tensor("shuT", [C, 2048], F32, kind="ExternalInput").ap()
    d_shd = nc.dram_tensor("shdT", [2048, C], F32, kind="ExternalInput").ap()
    d_ident = nc.dram_tensor("ident", [128, 128], BF16, kind="ExternalInput").ap()
    d_ctab = nc.dram_tensor("ctab", [128, T], BF16, kind="ExternalInput").ap()
    d_stab = nc.dram_tensor("stab", [128, T], BF16, kind="ExternalInput").ap()
    d_masks = nc.dram_tensor("masks", [128, 256], BF16, kind="ExternalInput").ap()
    d_out = nc.dram_tensor("out", [T, C], F32, kind="ExternalOutput").ap()
    d_acc = nc.dram_tensor("acc_spill", [T, C], BF16, kind="Internal").ap()
    d_q = nc.dram_tensor("q_scr", [H, 128, T], BF16, kind="Internal").ap()

    with ExitStack() as ctx:
        e = ctx.enter_context
        A_xn = e(nc.sbuf_tensor("A_xn", [128, NCC * T], BF16))
        A_big = e(nc.sbuf_tensor("A_big", [128, NCC * T], BF16))
        A_s1 = e(nc.sbuf_tensor("A_s1", [128, 16384], BF16))
        A_s2 = e(nc.sbuf_tensor("A_s2", [128, 16384], BF16))
        s_ident = e(nc.sbuf_tensor("s_ident", [128, 128], BF16))
        s_ctab = e(nc.sbuf_tensor("s_ctab", [128, T], BF16))
        s_stab = e(nc.sbuf_tensor("s_stab", [128, T], BF16))
        s_masks = e(nc.sbuf_tensor("s_masks", [128, 256], BF16))
        s_gw = e(nc.sbuf_tensor("s_gw", [128, NCC * 9], BF16))
        s_stats = e(nc.sbuf_tensor("s_stats", [128, 8], F32))
        s_sc9 = e(nc.sbuf_tensor("s_sc9", [128, 16], F32))
        s_top8 = e(nc.sbuf_tensor("s_top8", [128, 8], F32))
        s_dd = e(nc.sbuf_tensor("s_dd", [128, 2], F32))
        s_ee = e(nc.sbuf_tensor("s_ee", [128, 2], F32))
        s_w12 = e(nc.sbuf_tensor("s_w12", [128, 2], F32))
        s_eq = e(nc.sbuf_tensor("s_eq", [128, 16], F32))
        s_comb = e(nc.sbuf_tensor("s_comb", [128, NT * NE], F32))
        s_sgate = e(nc.sbuf_tensor("s_sgate", [128, NT], F32))
        s_nmax = e(nc.sbuf_tensor("s_nmax", [128, 4], F32))
        s_sum = e(nc.sbuf_tensor("s_sum", [128, 4], F32))
        s_inv = e(nc.sbuf_tensor("s_inv", [128, 4], F32))
        s_tmp = e(nc.sbuf_tensor("s_tmp", [128, 1024], BF16))
        s_dg = e(nc.sbuf_tensor("s_dg", [128, 384], BF16))

        PS1 = e(nc.psum_tensor("PS1", [128, 1024], F32))
        PS2 = e(nc.psum_tensor("PS2", [128, 1024], F32))
        PT = e(nc.psum_tensor("PT", [128, 512], F32))
        PTbf = None
        PAV = e(nc.psum_tensor("PAV", [128, 512], F32))
        PO = e(nc.psum_tensor("PO", [128, 1024], F32))

        for s in ("ld", "wt", "st", "te", "ve", "se"):
            mk.semh[s] = e(nc.semaphore(s))

        xn = A_xn[:].rearrange("p (c t) -> p c t", c=NCC)
        acc = A_big[:].rearrange("p (t o) -> p t o", t=NT)
        gwv = s_gw[:].rearrange("p (c n) -> p c n", c=NCC)
        s2f = A_s2[:].bitcast(F32)
        PS1bf = PS1[:].bitcast(BF16)
        PTbf = PT[:].bitcast(BF16)

        D_ = mk.do
        W_ = mk.w

        t_ident = D_("sync", lambda E: E.dma_start(out=s_ident[:], in_=d_ident), "ld", 16)
        t_ctab = D_("sync", lambda E: E.dma_start(out=s_ctab[:], in_=d_ctab), "ld", 16)
        t_stab = D_("sync", lambda E: E.dma_start(out=s_stab[:], in_=d_stab), "ld", 16)
        t_masks = D_("sync", lambda E: E.dma_start(out=s_masks[:], in_=d_masks), "ld", 16)
        t_gw = None
        for cc in range(NCC):
            t_gw = D_("gpsimd", lambda E, cc=cc: E.dma_start(
                out=gwv[:, cc, :], in_=d_gw[128 * cc:128 * (cc + 1), :]), "wt", 16)

        xbuf = [s2f[:, 0:2048], s2f[:, 2048:4096]]
        sqd = s2f[:, 4096:6144]
        xnt = [A_s2[:, 12288:14336], A_s2[:, 14336:16384]]
        xbuf_free = [None, None]
        xnt_free = [None, None]
        tr_dst = [PTbf[:, 0:512], PS1bf[:, 0:512]]
        tr_reg = [["pt"], ["ps1a"]]
        acc_init = [None] * NT
        gate_tok = [None] * NT
        sc9_free = None

        for tt in range(NT):
            b = tt % 2
            W_("sync", xbuf_free[b])
            t_x = D_("sync", lambda E, tt=tt, b=b: E.dma_start(
                out=xbuf[b], in_=d_x[128 * tt:128 * (tt + 1), :]), "ld", 16)
            acc_init[tt] = D_("gpsimd", lambda E, tt=tt: E.dma_start(
                out=acc[:, tt, :], in_=d_x[128 * tt:128 * (tt + 1), :]), "wt", 16)
            W_("scalar", t_x)
            t_sq = D_("scalar", lambda E, b=b: E.activation(
                sqd, xbuf[b], ACT.Square, accum_out=s_stats[:, 1:2]), "se")
            W_("vector", t_x)
            t_v = D_("vector", lambda E, b=b: E.reduce_sum(
                s_stats[:, 0:1], xbuf[b], axis=AX, negate=True), "ve")
            W_("vector", t_sq, t_v)
            t_v = D_("vector", lambda E: E.tensor_scalar(
                s_stats[:, 2:3], s_stats[:, 0:1], 1.0 / C, None, ALU.mult), "ve")
            W_("vector", t_v)
            t_v = D_("vector", lambda E: E.tensor_tensor(
                s_stats[:, 3:4], s_stats[:, 2:3], s_stats[:, 2:3], ALU.mult), "ve")
            t_v2 = D_("vector", lambda E: E.tensor_scalar(
                s_stats[:, 4:5], s_stats[:, 1:2], 1.0 / C, None, ALU.mult), "ve")
            W_("vector", t_v, t_v2)
            t_v = D_("vector", lambda E: E.tensor_tensor(
                s_stats[:, 4:5], s_stats[:, 4:5], s_stats[:, 3:4], ALU.subtract), "ve")
            W_("vector", t_v)
            t_v = D_("vector", lambda E: E.tensor_scalar_add(
                s_stats[:, 4:5], s_stats[:, 4:5], EPS), "ve")
            W_("scalar", t_v)
            t_s = D_("scalar", lambda E: E.activation(
                s_stats[:, 5:6], s_stats[:, 4:5], ACT.Ln), "se")
            W_("scalar", t_s)
            t_s = D_("scalar", lambda E: E.activation(
                s_stats[:, 6:7], s_stats[:, 5:6], ACT.Exp, scale=-0.5), "se")
            W_("vector", t_s)
            t_v = D_("vector", lambda E: E.tensor_tensor(
                s_stats[:, 7:8], s_stats[:, 2:3], s_stats[:, 6:7], ALU.mult), "ve")
            W_("scalar", t_v, xnt_free[b])
            t_xn = D_("scalar", lambda E, b=b: E.activation(
                xnt[b], xbuf[b], ACT.Identity, bias=s_stats[:, 7:8],
                scale=s_stats[:, 6:7]), "se")
            xbuf_free[b] = t_xn
            last_evac = None
            for g in range(4):
                rg = tr_reg[g % 2]
                mk.mm_into(rg)
                W_("tensor", t_xn, t_ident)
                t_t = None
                for j in range(4):
                    cc = 4 * g + j
                    t_t = D_("tensor", lambda E, b=b, cc=cc, g=g, j=j: E.transpose(
                        tr_dst[g % 2][:, 128 * j:128 * (j + 1)],
                        xnt[b][:, 128 * cc:128 * (cc + 1)], s_ident[:]),
                        "te" if j == 3 else None)
                mk.mm_done(rg, t_t)
                mk.evac_wait("vector", rg)
                t_ev = D_("vector", lambda E, g=g, tt=tt: E.tensor_copy(
                    xn[:, 4 * g:4 * g + 4, 128 * tt:128 * (tt + 1)],
                    tr_dst[g % 2][:].rearrange("p (c t) -> p c t", c=4)), "ve")
                mk.evac_done(rg, t_ev)
                last_evac = t_ev
            xnt_free[b] = mk.tok("te")
            mk.mm_into(["pav"])
            W_("tensor", last_evac, t_gw)
            t_mm = None
            for cc in range(NCC):
                t_mm = D_("tensor", lambda E, cc=cc, tt=tt: E.matmul(
                    PAV[:, 0:9], xn[:, cc, 128 * tt:128 * (tt + 1)], gwv[:, cc, :],
                    start=(cc == 0), stop=(cc == NCC - 1)),
                    "te" if cc == NCC - 1 else None)
            mk.mm_done(["pav"], t_mm)
            mk.evac_wait("vector", ["pav"])
            W_("vector", sc9_free)
            t_cp = D_("vector", lambda E: E.tensor_copy(s_sc9[:, 0:9], PAV[:, 0:9]), "ve")
            mk.evac_done(["pav"], t_cp)
            W_("vector", t_cp)
            t_m8 = D_("vector", lambda E: E.max(out=s_top8[:], in_=s_sc9[:, 0:8]), "ve")
            W_("vector", t_m8)
            t_d = D_("vector", lambda E: E.tensor_tensor(
                s_dd[:, 0:1], s_top8[:, 0:1], s_top8[:, 1:2], ALU.subtract), "ve")
            W_("vector", t_d)
            t_d2 = D_("vector", lambda E: E.tensor_scalar(
                s_dd[:, 1:2], s_dd[:, 0:1], -1.0, None, ALU.mult), "ve")
            W_("scalar", t_m8, t_d2)
            t_se1 = D_("scalar", lambda E: E.activation(
                s_ee[:, 0:2], s_dd[:, 0:2], ACT.Exp, scale=-1.0), "se")
            t_se2 = D_("scalar", lambda E: E.activation(
                s_sc9[:, 9:10], s_sc9[:, 8:9], ACT.Exp, scale=-1.0), "se")
            sc9_free = t_se2
            W_("vector", t_se1, t_se2)
            t_v = D_("vector", lambda E: E.tensor_scalar_add(
                s_ee[:, 0:2], s_ee[:, 0:2], 1.0), "ve")
            t_v2 = D_("vector", lambda E: E.tensor_scalar_add(
                s_sc9[:, 9:10], s_sc9[:, 9:10], 1.0), "ve")
            W_("vector", t_v, t_v2)
            t_w = D_("vector", lambda E: E.reciprocal(s_w12[:, 0:2], s_ee[:, 0:2]), "ve")
            t_sg = D_("vector", lambda E, tt=tt: E.reciprocal(
                s_sgate[:, tt:tt + 1], s_sc9[:, 9:10]), "ve")
            W_("vector", t_w, t_sg)
            t_e1 = D_("vector", lambda E: E.tensor_single_scalar(
                s_eq[:, 0:8], s_sc9[:, 0:8], s_top8[:, 0:1], ALU.is_equal), "ve")
            t_e2 = D_("vector", lambda E: E.tensor_single_scalar(
                s_eq[:, 8:16], s_sc9[:, 0:8], s_top8[:, 1:2], ALU.is_equal), "ve")
            W_("vector", t_e1, t_e2)
            t_v = D_("vector", lambda E, tt=tt: E.tensor_scalar(
                s_comb[:, 8 * tt:8 * tt + 8], s_eq[:, 0:8], s_w12[:, 0:1], None,
                ALU.mult), "ve")
            W_("vector", t_v)
            gate_tok[tt] = D_("vector", lambda E, tt=tt: E.scalar_tensor_tensor(
                s_comb[:, 8 * tt:8 * tt + 8], s_eq[:, 8:16], s_w12[:, 1:2],
                s_comb[:, 8 * tt:8 * tt + 8], ALU.mult, ALU.add), "ve")

        xn_done = mk.tok("ve")
        p1_done_te = mk.tok("te")
        p1_done_se = mk.tok("se")
        acc_ready = mk.tok("wt")

        shg_b = [A_s2[:, 0:2048], A_s2[:, 2048:4096]]
        shu_b = [A_s2[:, 4096:6144], A_s2[:, 6144:8192]]
        shd_b = [A_s2[:, 8192:12288], A_s2[:, 12288:16384]]
        hid = A_s1[:].rearrange("p (h t) -> p h t", h=8)
        gu_reg = [[["ps1a"], ["ps1b"]], [["ps2a"], ["ps2b"]]]
        dn_reg = [["po0"], ["po1"]]
        dn_dst = [PO[:, 0:512], PO[:, 512:1024]]
        shg_free = [[xn_done, p1_done_te, p1_done_se], [xn_done, p1_done_te, p1_done_se]]
        shu_free = [list(shg_free[0]), list(shg_free[1])]
        shd_free = [list(shg_free[0]), list(shg_free[1])]
        stp_free = [None, None]
        hid_wtok = None

        srcg_sh = d_shg.rearrange("(cc p) n -> p cc n", p=128)
        srcu_sh = d_shu.rearrange("(cc p) n -> p cc n", p=128)
        srcd_sh = d_shd.rearrange("(hh p) n -> p hh n", p=128)

        for half in range(2):
            for hc in range(8):
                hsc = 8 * half + hc
                b = hc % 2
                W_("gpsimd", shg_free[b], shu_free[b])
                t_g = t_u = None
                for q4 in range(4):
                    t_g = D_("gpsimd", lambda E, hsc=hsc, q4=q4, b=b: E.dma_start(
                        out=shg_b[b].rearrange("p (cc n) -> p cc n", cc=NCC)[:, 4 * q4:4 * q4 + 4, :],
                        in_=srcg_sh[:, 4 * q4:4 * q4 + 4, 128 * hsc:128 * (hsc + 1)]), "wt", 16)
                    t_u = D_("gpsimd", lambda E, hsc=hsc, q4=q4, b=b: E.dma_start(
                        out=shu_b[b].rearrange("p (cc n) -> p cc n", cc=NCC)[:, 4 * q4:4 * q4 + 4, :],
                        in_=srcu_sh[:, 4 * q4:4 * q4 + 4, 128 * hsc:128 * (hsc + 1)]), "wt", 16)
                for tc in range(NTC):
                    par = (hc * NTC + tc) % 2
                    rgu = gu_reg[0][par] + gu_reg[1][par]
                    pg = PS1[:, 512 * par:512 * (par + 1)]
                    pu = PS2[:, 512 * par:512 * (par + 1)]
                    mk.mm_into(rgu)
                    W_("tensor", t_g, t_u, xn_done)
                    t_mm = None
                    for cc in range(NCC):
                        D_("tensor", lambda E, cc=cc, tc=tc, b=b, pg=pg: E.matmul(
                            pg, shg_b[b][:, 128 * cc:128 * (cc + 1)],
                            xn[:, cc, 512 * tc:512 * (tc + 1)],
                            start=(cc == 0), stop=(cc == NCC - 1)))
                    for cc in range(NCC):
                        t_mm = D_("tensor", lambda E, cc=cc, tc=tc, b=b, pu=pu: E.matmul(
                            pu, shu_b[b][:, 128 * cc:128 * (cc + 1)],
                            xn[:, cc, 512 * tc:512 * (tc + 1)],
                            start=(cc == 0), stop=(cc == NCC - 1)),
                            "te" if cc == NCC - 1 else None)
                    mk.mm_done(rgu, t_mm)
                    if tc == NTC - 1:
                        shg_free[b] = [t_mm]
                        shu_free[b] = [t_mm]
                    stp = s_tmp[:, 512 * par:512 * (par + 1)]
                    mk.evac_wait("scalar", gu_reg[0][par])
                    W_("scalar", stp_free[par])
                    t_s = D_("scalar", lambda E, stp=stp, pg=pg: E.activation(
                        stp, pg, ACT.Silu), "se")
                    mk.evac_done(gu_reg[0][par], t_s)
                    mk.evac_wait("vector", gu_reg[1][par])
                    W_("vector", t_s)
                    hid_wtok = D_("vector", lambda E, hc=hc, tc=tc, stp=stp, pu=pu: E.tensor_tensor(
                        hid[:, hc, 512 * tc:512 * (tc + 1)], stp, pu, ALU.mult), "ve")
                    mk.evac_done(gu_reg[1][par], hid_wtok)
                    stp_free[par] = hid_wtok
            hid_done = hid_wtok
            for oc in range(NTC):
                b = oc % 2
                W_("gpsimd", shd_free[b])
                t_d = None
                for q2 in range(2):
                    t_d = D_("gpsimd", lambda E, half=half, oc=oc, b=b, q2=q2: E.dma_start(
                        out=shd_b[b].rearrange("p (hh n) -> p hh n", hh=8)[:, 4 * q2:4 * q2 + 4, :],
                        in_=srcd_sh[:, 8 * half + 4 * q2:8 * half + 4 * q2 + 4,
                                    512 * oc:512 * (oc + 1)]), "wt", 16)
                for tt in range(NT):
                    par = (oc * NT + tt) % 2
                    rr = dn_reg[par]
                    pd = dn_dst[par]
                    mk.mm_into(rr)
                    W_("tensor", t_d, hid_done)
                    t_mm = None
                    for hc in range(8):
                        t_mm = D_("tensor", lambda E, hc=hc, tt=tt, b=b, pd=pd: E.matmul(
                            pd[:, 0:512], hid[:, hc, 128 * tt:128 * (tt + 1)],
                            shd_b[b][:, 512 * hc:512 * (hc + 1)],
                            start=(hc == 0), stop=(hc == 7)),
                            "te" if hc == 7 else None)
                    mk.mm_done(rr, t_mm)
                    if tt == NT - 1:
                        shd_free[b] = [t_mm]
                    mk.evac_wait("vector", rr)
                    W_("vector", gate_tok[tt], acc_ready)
                    t_v = D_("vector", lambda E, tt=tt, oc=oc, pd=pd: E.scalar_tensor_tensor(
                        acc[:, tt, 512 * oc:512 * (oc + 1)], pd[:, 0:512],
                        s_sgate[:, tt:tt + 1], acc[:, tt, 512 * oc:512 * (oc + 1)],
                        ALU.mult, ALU.add), "ve")
                    mk.evac_done(rr, t_v)
        shared_done_te = mk.tok("te")

        wg_b = shg_b
        wu_b = shu_b
        wd_b = shd_b
        hh = hid
        wg_free = [[shared_done_te], [shared_done_te]]
        wu_free = [[shared_done_te], [shared_done_te]]
        wd_free = [[shared_done_te], [shared_done_te]]
        h_free = [shared_done_te]
        h_done = None
        srcg_e = d_wg.rearrange("e (cc p) n -> e p cc n", p=128)
        srcu_e = d_wu.rearrange("e (cc p) n -> e p cc n", p=128)
        srcd_e = d_wd.rearrange("e (hh p) n -> e p hh n", p=128)

        for ex in range(NE):
            for hc in range(NHC):
                b = hc % 2
                W_("gpsimd", wg_free[b], wu_free[b])
                t_g = t_u = None
                for q4 in range(4):
                    t_g = D_("gpsimd", lambda E, ex=ex, hc=hc, q4=q4, b=b: E.dma_start(
                        out=wg_b[b].rearrange("p (cc n) -> p cc n", cc=NCC)[:, 4 * q4:4 * q4 + 4, :],
                        in_=srcg_e[ex, :, 4 * q4:4 * q4 + 4, 128 * hc:128 * (hc + 1)]), "wt", 16)
                    t_u = D_("gpsimd", lambda E, ex=ex, hc=hc, q4=q4, b=b: E.dma_start(
                        out=wu_b[b].rearrange("p (cc n) -> p cc n", cc=NCC)[:, 4 * q4:4 * q4 + 4, :],
                        in_=srcu_e[ex, :, 4 * q4:4 * q4 + 4, 128 * hc:128 * (hc + 1)]), "wt", 16)
                for tc in range(NTC):
                    par = (hc * NTC + tc) % 2
                    rgu = gu_reg[0][par] + gu_reg[1][par]
                    pg = PS1[:, 512 * par:512 * (par + 1)]
                    pu = PS2[:, 512 * par:512 * (par + 1)]
                    mk.mm_into(rgu)
                    W_("tensor", t_g, t_u, h_free)
                    t_mm = None
                    for cc in range(NCC):
                        D_("tensor", lambda E, cc=cc, tc=tc, b=b, pg=pg: E.matmul(
                            pg, wg_b[b][:, 128 * cc:128 * (cc + 1)],
                            xn[:, cc, 512 * tc:512 * (tc + 1)],
                            start=(cc == 0), stop=(cc == NCC - 1)))
                    for cc in range(NCC):
                        t_mm = D_("tensor", lambda E, cc=cc, tc=tc, b=b, pu=pu: E.matmul(
                            pu, wu_b[b][:, 128 * cc:128 * (cc + 1)],
                            xn[:, cc, 512 * tc:512 * (tc + 1)],
                            start=(cc == 0), stop=(cc == NCC - 1)),
                            "te" if cc == NCC - 1 else None)
                    mk.mm_done(rgu, t_mm)
                    if tc == NTC - 1:
                        wg_free[b] = [t_mm]
                        wu_free[b] = [t_mm]
                    stp = s_tmp[:, 512 * par:512 * (par + 1)]
                    mk.evac_wait("scalar", gu_reg[0][par])
                    W_("scalar", stp_free[par])
                    t_s = D_("scalar", lambda E, stp=stp, pg=pg: E.activation(
                        stp, pg, ACT.Silu), "se")
                    mk.evac_done(gu_reg[0][par], t_s)
                    mk.evac_wait("vector", gu_reg[1][par])
                    W_("vector", t_s)
                    h_done = D_("vector", lambda E, hc=hc, tc=tc, stp=stp, pu=pu: E.tensor_tensor(
                        hh[:, hc, 512 * tc:512 * (tc + 1)], stp, pu, ALU.mult), "ve")
                    mk.evac_done(gu_reg[1][par], h_done)
                    stp_free[par] = h_done
            for oc in range(NTC):
                b = oc % 2
                W_("gpsimd", wd_free[b])
                t_d = None
                for q2 in range(2):
                    t_d = D_("gpsimd", lambda E, ex=ex, oc=oc, b=b, q2=q2: E.dma_start(
                        out=wd_b[b].rearrange("p (hh n) -> p hh n", hh=8)[:, 4 * q2:4 * q2 + 4, :],
                        in_=srcd_e[ex, :, 4 * q2:4 * q2 + 4, 512 * oc:512 * (oc + 1)]), "wt", 16)
                for tt in range(NT):
                    par = (oc * NT + tt) % 2
                    rr = dn_reg[par]
                    pd = dn_dst[par]
                    mk.mm_into(rr)
                    W_("tensor", t_d, h_done)
                    t_mm = None
                    for hc in range(8):
                        t_mm = D_("tensor", lambda E, hc=hc, tt=tt, b=b, pd=pd: E.matmul(
                            pd[:, 0:512], hh[:, hc, 128 * tt:128 * (tt + 1)],
                            wd_b[b][:, 512 * hc:512 * (hc + 1)],
                            start=(hc == 0), stop=(hc == 7)),
                            "te" if hc == 7 else None)
                    mk.mm_done(rr, t_mm)
                    if tt == NT - 1:
                        wd_free[b] = [t_mm]
                    stp = s_tmp[:, 512 * par:512 * (par + 1)]
                    mk.evac_wait("scalar", rr)
                    W_("scalar", gate_tok[tt], stp_free[par])
                    t_s = D_("scalar", lambda E, tt=tt, ex=ex, stp=stp, pd=pd: E.activation(
                        stp, pd[:, 0:512], ACT.Identity,
                        scale=s_comb[:, 8 * tt + ex:8 * tt + ex + 1]), "se")
                    mk.evac_done(rr, t_s)
                    W_("vector", t_s)
                    t_v = D_("vector", lambda E, tt=tt, oc=oc, stp=stp: E.tensor_tensor(
                        acc[:, tt, 512 * oc:512 * (oc + 1)], stp,
                        acc[:, tt, 512 * oc:512 * (oc + 1)], ALU.add), "ve")
                    stp_free[par] = t_v
            h_free = [mk.tok("te")]
        ffn_done_v = mk.tok("ve")
        ffn_done_te = mk.tok("te")

        spill = [None] * NT
        W_("sync", ffn_done_v)
        for tt in range(NT):
            spill[tt] = D_("sync", lambda E, tt=tt: E.dma_start(
                out=d_acc[128 * tt:128 * (tt + 1), :], in_=acc[:, tt, :]), "st", 16)
        spill_done = mk.tok("st")

        Ksb = A_big[:, 0:8192].rearrange("p (kv t) -> p kv t", kv=KV)
        Vsb = A_big[:, 8192:16384].rearrange("p (t n) -> p t n", t=NT)
        Qsb = A_big[:, 16384:24576].rearrange("p (h t) -> p h t", h=H)
        AVs = A_big[:, 24576:32768].rearrange("p (q n) -> p q n", q=4)
        wk_sb = A_s1[:, 0:8192].rearrange("p (cc n) -> p cc n", cc=NCC)
        wv_sb = A_s1[:, 8192:16384].rearrange("p (cc n) -> p cc n", cc=NCC)
        W_("gpsimd", ffn_done_te)
        t_wk = t_wv = None
        srck = d_wk.rearrange("(cc p) n -> p cc n", p=128)
        srcv = d_wv.rearrange("(cc p) n -> p cc n", p=128)
        for q4 in range(4):
            t_wk = D_("gpsimd", lambda E, q4=q4: E.dma_start(
                out=wk_sb[:, 4 * q4:4 * q4 + 4, :],
                in_=srck[:, 4 * q4:4 * q4 + 4, :]), "wt", 16)
            t_wv = D_("gpsimd", lambda E, q4=q4: E.dma_start(
                out=wv_sb[:, 4 * q4:4 * q4 + 4, :],
                in_=srcv[:, 4 * q4:4 * q4 + 4, :]), "wt", 16)

        wq_b = [A_s2[:, 0:2048], A_s2[:, 2048:4096]]
        qraw = [A_s2[:, 4096:4608], A_s2[:, 4608:5120]]
        qswp = [A_s2[:, 5120:5632], A_s2[:, 5632:6144]]
        p_un3 = [A_s2[:, 6144:6784], A_s2[:, 10496:11136], A_s2[:, 11136:11776]]
        pT_sb = A_s2[:, 6784:7424]
        accj = [A_s2[:, 7424:7936], A_s2[:, 7936:8448]]
        out_b = [s2f[:, 4224:4736], s2f[:, 4736:5248]]
        rtA = [A_s2[:, 13184:13696], A_s2[:, 13696:14208]]
        rtB = [A_s2[:, 14208:14720], A_s2[:, 14720:15232]]
        qraw_free = [None, None]
        qswp_free = [None, None]
        rt_free = [None, None]
        kq_reg = ["ps1a", "ps1b", "ps2a", "ps2b"]
        kq_dst = [PS1[:, 0:512], PS1[:, 512:1024], PS2[:, 0:512], PS2[:, 512:1024]]
        rot_i = [0]

        def rope_project(dst_ap, wslice, t0, extra):
            par = rot_i[0] % 4
            rb = rot_i[0] % 2
            rot_i[0] += 1
            rg = [kq_reg[par]]
            pq = kq_dst[par]
            mk.mm_into(rg)
            W_("tensor", spill_done, xn_done, extra)
            t_mm = None
            for cc in range(NCC):
                t_mm = D_("tensor", lambda E, cc=cc, pq=pq, t0=t0: E.matmul(
                    pq, wslice(cc), xn[:, cc, t0:t0 + 512],
                    start=(cc == 0), stop=(cc == NCC - 1)),
                    "te" if cc == NCC - 1 else None)
            mk.mm_done(rg, t_mm)
            mk.evac_wait("scalar", rg)
            W_("scalar", qraw_free[rb])
            t_s = D_("scalar", lambda E, rb=rb, pq=pq: E.copy(qraw[rb], pq), "se")
            mk.evac_done(rg, t_s)
            W_("sync", t_s, qswp_free[rb])
            D_("sync", lambda E, rb=rb: E.dma_start(
                out=qswp[rb][0:64, :], in_=qraw[rb][64:128, :]), "st", 16)
            t_sw = D_("sync", lambda E, rb=rb: E.dma_start(
                out=qswp[rb][64:128, :], in_=qraw[rb][0:64, :]), "st", 16)
            W_("vector", t_s, rt_free[rb], t_ctab, t_stab)
            t_a = D_("vector", lambda E, rb=rb, t0=t0: E.tensor_tensor(
                rtA[rb], qraw[rb], s_ctab[:, t0:t0 + 512], ALU.mult), "ve")
            qraw_free[rb] = t_a
            W_("vector", t_sw)
            t_b2 = D_("vector", lambda E, rb=rb, t0=t0: E.tensor_tensor(
                rtB[rb], qswp[rb], s_stab[:, t0:t0 + 512], ALU.mult), "ve")
            qswp_free[rb] = t_b2
            W_("vector", t_a, t_b2)
            t_o = D_("vector", lambda E, rb=rb, dst=dst_ap: E.tensor_tensor(
                dst, rtA[rb], rtB[rb], ALU.add), "ve")
            rt_free[rb] = t_o
            return t_o

        k_tok = [None] * NT
        v_tok = [None] * NT
        for tc in range(NTC):
            t_o = None
            for kv in range(KV):
                t_o = rope_project(
                    Ksb[:, kv, 512 * tc:512 * (tc + 1)],
                    (lambda cc, kv=kv: wk_sb[:, cc, 128 * kv:128 * (kv + 1)]),
                    512 * tc, t_wk)
            for j in range(4):
                k_tok[4 * tc + j] = t_o
        vrot = [0]
        for tt in range(NT):
            par = vrot[0] % 2
            vrot[0] += 1
            rr = dn_reg[par]
            pv = dn_dst[par]
            mk.mm_into(rr)
            W_("tensor", spill_done, t_wv, xn_done)
            t_mm = None
            for cc in range(NCC):
                t_mm = D_("tensor", lambda E, cc=cc, tt=tt, pv=pv: E.matmul(
                    pv[:, 0:512], xn[:, cc, 128 * tt:128 * (tt + 1)], wv_sb[:, cc, :],
                    start=(cc == 0), stop=(cc == NCC - 1)),
                    "te" if cc == NCC - 1 else None)
            mk.mm_done(rr, t_mm)
            mk.evac_wait("scalar", rr)
            t_s = D_("scalar", lambda E, tt=tt, pv=pv: E.copy(Vsb[:, tt, :], pv[:, 0:512]), "se")
            mk.evac_done(rr, t_s)
            v_tok[tt] = t_s

        wo_sb = [A_s1[:, 0:8192].rearrange("p (hd n) -> p hd n", hd=NCC),
                 A_s1[:, 8192:16384].rearrange("p (hd n) -> p hd n", hd=NCC)]
        kv_mm_done = mk.tok("te")
        wo_free = [[kv_mm_done], [kv_mm_done]]
        wq_free = [[ffn_done_te], [ffn_done_te]]
        accj_free = [None, None]
        out_free = [None, None]
        pun_free = [None, None, None]
        pT_free = [None]
        AVs8 = A_big[:, 16384:32768].rearrange("p (s n) -> p s n", s=8)
        av_ready = [None] * 8
        av_slot_free = [spill_done] * 8
        nmax_free = [None, None, None]
        sc_reg2 = [["ps1a", "ps1b"], ["ps2a", "ps2b"]]
        sc_dst2 = [PS1, PS2]
        qa_free = spill_done
        srcq = d_wq.rearrange("(cc p) n -> p cc n", p=128)
        srco = d_wo.rearrange("(hd p) n -> p hd n", p=128)
        qstage = [A_s2[:, 11776:12288], A_s2[:, 12288:12800]]
        qstage_free = [None, None]
        qjit = [A_s2[:, 15232 + 128 * i:15232 + 128 * (i + 1)] for i in range(4)]
        qjit_free = [[spill_done] for _ in range(4)]
        q_st_tok = [None] * H

        o_jobs = []
        drained = {}

        def o_load(job):
            if job["t_wo"] is not None:
                return
            oc, wb = job["oc"], job["wb"]
            W_("gpsimd", wo_free[wb])
            t_wo = None
            for q4 in range(4):
                t_wo = D_("gpsimd", lambda E, oc=oc, wb=wb, q4=q4: E.dma_start(
                    out=wo_sb[wb][:, 4 * q4:4 * q4 + 4, :],
                    in_=srco[:, 4 * q4:4 * q4 + 4, 512 * oc:512 * (oc + 1)]), "wt", 16)
            job["t_wo"] = t_wo

        def o_create(tcs):
            avp = (tcs % 2) * 4
            for oc in range(NTC):
                o_jobs.append(dict(tcs=tcs, oc=oc, wb=oc % 2, t_wo=None, step=0,
                                   avp=avp, t_acc=None))
            o_load(o_jobs[0])
            if len(o_jobs) > 1:
                o_load(o_jobs[1])

        def o_pump():
            if not o_jobs:
                return
            job = o_jobs[0]
            if job["step"] // 4 > drained.get(job["tcs"], -1):
                return
            o_load(job)
            if job["step"] == 0 and len(o_jobs) > 2:
                o_load(o_jobs[2])
            st_ = job["step"]
            job["step"] += 1
            if job["step"] == 16:
                o_jobs.pop(0)
            oc, wb, tcs, avp = job["oc"], job["wb"], job["tcs"], job["avp"]
            jjp = st_ // 4
            part = st_ % 4
            qi = 4 * tcs + jjp
            slot = avp + jjp
            par = (oc * 4 + jjp) % 2
            rr = dn_reg[par]
            po = dn_dst[par]
            if part == 0:
                W_("sync", accj_free[par], spill[qi])
                job["t_acc"] = D_("sync", lambda E, qi=qi, oc=oc, par=par: E.dma_start(
                    out=accj[par], in_=d_acc[128 * qi:128 * (qi + 1),
                                             512 * oc:512 * (oc + 1)]), "ld", 16)
                mk.mm_into(rr)
                W_("tensor", av_ready[slot], job["t_wo"])
            t_mm = None
            for h in range(4 * part, 4 * part + 4):
                t_mm = D_("tensor", lambda E, h=h, slot=slot, wb=wb, po=po: E.matmul(
                    po[:, 0:512], AVs8[:, slot, 128 * h:128 * (h + 1)],
                    wo_sb[wb][:, h, :],
                    start=(h == 0), stop=(h == H - 1)),
                    "te" if h == H - 1 else None)
            if part == 3:
                mk.mm_done(rr, t_mm)
                if jjp == 3:
                    wo_free[wb] = [t_mm]
                if oc == NTC - 1:
                    av_slot_free[slot] = t_mm
                mk.evac_wait("vector", rr)
                W_("vector", job["t_acc"], out_free[par])
                t_v = D_("vector", lambda E, par=par, po=po: E.tensor_tensor(
                    out_b[par], po[:, 0:512], accj[par], ALU.add), "ve")
                mk.evac_done(rr, t_v)
                accj_free[par] = t_v
                W_("sync", t_v)
                t_st = D_("sync", lambda E, qi=qi, oc=oc, par=par: E.dma_start(
                    out=d_out[128 * qi:128 * (qi + 1), 512 * oc:512 * (oc + 1)],
                    in_=out_b[par]), "st", 16)
                out_free[par] = t_st

        for tc in range(NTC):
            for h in range(H):
                b = h % 2
                W_("gpsimd", wq_free[b])
                t_wq = None
                for q4 in range(4):
                    t_wq = D_("gpsimd", lambda E, h=h, q4=q4, b=b: E.dma_start(
                        out=wq_b[b].rearrange("p (cc n) -> p cc n", cc=NCC)[:, 4 * q4:4 * q4 + 4, :],
                        in_=srcq[:, 4 * q4:4 * q4 + 4, 128 * h:128 * (h + 1)]), "wt", 16)
                sb = h % 2
                t_o = rope_project(
                    qstage[sb],
                    (lambda cc, b=b: wq_b[b][:, 128 * cc:128 * (cc + 1)]),
                    512 * tc, [t_wq, qstage_free[sb]])
                W_("sync", t_o)
                t_qst = D_("sync", lambda E, h=h, tc=tc, sb=sb: E.dma_start(
                    out=d_q[h, :, 512 * tc:512 * (tc + 1)], in_=qstage[sb]), "st", 16)
                qstage_free[sb] = t_qst
                q_st_tok[h] = t_qst
                wq_free[b] = [mk.tok("te")]

            qld_tok = [None] * (4 * H)
            qld_i = [0]

            def q_load(tc=tc):
                idx = qld_i[0]
                if idx >= 4 * H:
                    return
                qld_i[0] += 1
                jj_, h_ = idx // H, idx % H
                qi_ = 4 * tc + jj_
                qb = idx % 4
                W_("sync", qjit_free[qb], q_st_tok[h_])
                qld_tok[idx] = D_("sync", lambda E, h_=h_, qi_=qi_, qb=qb: E.dma_start(
                    out=qjit[qb], in_=d_q[h_, :, 128 * qi_:128 * (qi_ + 1)]), "ld", 16)

            q_load()
            q_load()
            q_load()
            for jj in range(4):
                qi = 4 * tc + jj
                j0 = max(0, qi - WIN)
                nk = (qi - j0 + 1) * 128
                ntile = nk // 128
                slot = (tc % 2) * 4 + jj
                pend = []

                def stage2(jj=jj, j0=j0, nk=nk, ntile=ntile, qi=qi, slot=slot):
                    h, t_dg = pend.pop(0)
                    sps = h % 3
                    kvh = h // 4
                    pnb = p_un3[sps]
                    dgs = s_dg[:, 128 * sps:128 * (sps + 1)]
                    t_regs = ["pt"] + (["pav"] if ntile == 5 else [])
                    mk.mm_into(t_regs)
                    W_("tensor", t_dg)
                    t_t = None
                    for j in range(ntile):
                        dst = PT[:, 128 * j:128 * (j + 1)] if j < 4 else PAV[:, 128:256]
                        t_t = D_("tensor", lambda E, j=j, pnb=pnb, dgs=dgs, dst=dst: E.matmul(
                            dst, pnb[:, 128 * j:128 * (j + 1)],
                            dgs, start=True, stop=True),
                            "te" if j == ntile - 1 else None)
                    mk.mm_done(t_regs, t_t)
                    pun_free[sps] = t_t
                    mk.evac_wait("scalar", t_regs)
                    W_("scalar", pT_free[0])
                    t_ev = D_("scalar", lambda E, nk=nk: E.copy(
                        pT_sb[:, 0:min(nk, 512)], PT[:, 0:min(nk, 512)]), "se")
                    if ntile == 5:
                        t_ev = D_("scalar", lambda E: E.copy(
                            pT_sb[:, 512:640], PAV[:, 128:256]), "se")
                    mk.evac_done(t_regs, t_ev)
                    mk.mm_into(["pav"])
                    W_("tensor", t_ev, v_tok[qi])
                    t_mm = None
                    for j in range(ntile):
                        t_mm = D_("tensor", lambda E, j=j, j0=j0, kvh=kvh: E.matmul(
                            PAV[:, 0:128], Vsb[:, j0 + j, 128 * kvh:128 * (kvh + 1)],
                            pT_sb[:, 128 * j:128 * (j + 1)],
                            start=(j == 0), stop=(j == ntile - 1)),
                            "te" if j == ntile - 1 else None)
                    mk.mm_done(["pav"], t_mm)
                    pT_free[0] = t_mm
                    mk.evac_wait("vector", ["pav"])
                    W_("vector", av_slot_free[slot])
                    t_e = D_("vector", lambda E, slot=slot, h=h: E.tensor_copy(
                        AVs8[:, slot, 128 * h:128 * (h + 1)], PAV[:, 0:128]), "ve")
                    mk.evac_done(["pav"], t_e)
                    return t_e

                for h in range(H):
                    idx = jj * H + h
                    q_load()
                    spp = h % 2
                    sps = h % 3
                    rg = sc_reg2[spp]
                    ps = sc_dst2[spp]
                    kvh = h // 4
                    qb = idx % 4
                    mk.mm_into(rg)
                    W_("tensor", qld_tok[idx], k_tok[qi], t_masks)
                    nfull = min(nk, 512)
                    D_("tensor", lambda E, ps=ps, qb=qb, kvh=kvh, j0=j0, nfull=nfull: E.matmul(
                        ps[:, 0:nfull], qjit[qb],
                        Ksb[:, kvh, 128 * j0:128 * j0 + nfull],
                        start=True, stop=True), "te")
                    if nk > 512:
                        D_("tensor", lambda E, ps=ps, qb=qb, kvh=kvh, j0=j0, nk=nk: E.matmul(
                            ps[:, 512:nk], qjit[qb],
                            Ksb[:, kvh, 128 * j0 + 512:128 * j0 + nk],
                            start=True, stop=True), "te")
                    t_mm = D_("tensor", lambda E, ps=ps, nk=nk: E.matmul(
                        ps[:, nk - 128:nk], s_ident[:], s_masks[:, 0:128],
                        start=False, stop=True), "te")
                    if qi - j0 == WIN:
                        t_mm = D_("tensor", lambda E, ps=ps: E.matmul(
                            ps[:, 0:128], s_ident[:], s_masks[:, 128:256],
                            start=False, stop=True), "te")
                    mk.mm_done(rg, t_mm)
                    qjit_free[qb] = [t_mm]
                    mk.evac_wait("vector", rg)
                    W_("vector", nmax_free[sps])
                    t_v = D_("vector", lambda E, ps=ps, nk=nk, sps=sps: E.reduce_max(
                        s_nmax[:, sps:sps + 1], ps[:, nk - 128:nk], axis=AX, negate=True), "ve")
                    W_("scalar", t_v, pun_free[sps])
                    t_exp = D_("scalar", lambda E, ps=ps, nk=nk, sps=sps: E.activation(
                        p_un3[sps][:, 0:nk], ps[:, 0:nk], ACT.Exp,
                        bias=s_nmax[:, sps:sps + 1], accum_out=s_sum[:, sps:sps + 1]), "se")
                    mk.evac_done(rg, t_exp)
                    nmax_free[sps] = t_exp
                    W_("vector", t_exp)
                    t_inv = D_("vector", lambda E, sps=sps: E.reciprocal(
                        s_inv[:, sps:sps + 1], s_sum[:, sps:sps + 1]), "ve")
                    W_("vector", t_inv, pun_free[sps])
                    t_dg = D_("vector", lambda E, sps=sps: E.tensor_scalar(
                        s_dg[:, 128 * sps:128 * (sps + 1)], s_ident[:],
                        s_inv[:, sps:sps + 1], None, ALU.mult), "ve")
                    pend.append((h, t_dg))
                    o_pump()
                    if len(pend) >= 2:
                        stage2()
                while pend:
                    stage2()
                av_ready[slot] = mk.tok("se")
                av_ready_v = mk.tok("ve")
                av_ready[slot] = [av_ready[slot], av_ready_v]
                drained[tc] = jj
                if jj == 0:
                    o_create(tc)
            qa_free = mk.tok("te")

        while o_jobs:
            o_pump()

        D_("sync", lambda E, v=mk.cnt["st"]: E.wait_ge(mk.semh["st"], v))

        if emit:
            with nc.Block() as block:
                @block.sync
                def _(E):
                    for f in mk.ops["sync"]:
                        f(E)

                @block.gpsimd
                def _(E):
                    for f in mk.ops["gpsimd"]:
                        f(E)

                @block.tensor
                def _(E):
                    for f in mk.ops["tensor"]:
                        f(E)

                @block.vector
                def _(E):
                    for f in mk.ops["vector"]:
                        f(E)

                @block.scalar
                def _(E):
                    for f in mk.ops["scalar"]:
                        f(E)

    return nc


def _consts():
    inv = 10000.0 ** (-np.arange(0, 128, 2, dtype=np.float64) / 128.0)
    ang = np.arange(T, dtype=np.float64)[:, None] * inv[None, :]
    s4 = 128.0 ** -0.25
    cosd = np.cos(ang).T * s4
    sind = np.sin(ang).T * s4
    ctab = np.concatenate([cosd, cosd], 0).astype(ml_dtypes.bfloat16)
    stab = np.concatenate([-sind, sind], 0).astype(ml_dtypes.bfloat16)
    ident = np.eye(128, dtype=ml_dtypes.bfloat16)
    ii = np.arange(128)
    diag = np.where(ii[:, None] >= ii[None, :], 0.0, -1e9)
    far = np.where(ii[:, None] < ii[None, :], 0.0, -1e9)
    masks = np.concatenate([diag, far], 1).astype(ml_dtypes.bfloat16)
    return ident, ctab, stab, masks


def kernel(**inp):
    f32 = np.float32
    x = np.ascontiguousarray(np.asarray(inp["L0_attn_x"], f32))
    wqT = np.ascontiguousarray(np.asarray(inp["L0_attn_Wq"], f32).T)
    wkT = np.ascontiguousarray(np.asarray(inp["L0_attn_Wk"], f32).T)
    wvT = np.ascontiguousarray(np.asarray(inp["L0_attn_Wv"], f32).T)
    woT = np.ascontiguousarray(np.asarray(inp["L0_attn_Wo"], f32).T)
    gw9 = np.ascontiguousarray(np.concatenate(
        [np.asarray(inp["L0_ffn_gate_w"], f32),
         np.asarray(inp["L0_ffn_se_gate_w"], f32)], 0).T)
    wgT = np.ascontiguousarray(np.transpose(np.asarray(inp["L0_ffn_w_gate"], f32), (0, 2, 1)))
    wuT = np.ascontiguousarray(np.transpose(np.asarray(inp["L0_ffn_w_up"], f32), (0, 2, 1)))
    wdT = np.ascontiguousarray(np.transpose(np.asarray(inp["L0_ffn_w_down"], f32), (0, 2, 1)))
    shgT = np.ascontiguousarray(np.asarray(inp["L0_ffn_sh_gate"], f32).T)
    shuT = np.ascontiguousarray(np.asarray(inp["L0_ffn_sh_up"], f32).T)
    shdT = np.ascontiguousarray(np.asarray(inp["L0_ffn_sh_down"], f32).T)
    ident, ctab, stab, masks = _consts()

    nc = build_program()
    shared = dict(wqT=wqT, wkT=wkT, wvT=wvT, woT=woT, gw9=gw9, wgT=wgT,
                  wuT=wuT, wdT=wdT, shgT=shgT, shuT=shuT, shdT=shdT,
                  ident=ident, ctab=ctab, stab=stab, masks=masks)
    in_maps = [dict(x=np.ascontiguousarray(x[i]), **shared) for i in range(8)]
    res = run_bass_kernel_spmd(nc, in_maps, core_ids=list(range(8)))
    return np.stack([r["out"] for r in res.results], 0)
